# version 06
# baseline (speedup 1.0000x reference)
import contextlib
import math
import numpy as np
import concourse.bass as bass
import concourse.mybir as mybir
from concourse.bass_utils import run_bass_kernel_spmd

F32 = mybir.dt.float32
BF16 = mybir.dt.bfloat16
I32 = mybir.dt.int32
AF = mybir.ActivationFunctionType
ALU = mybir.AluOpType
AX = mybir.AxisListType

D = 4096
S = 4096
T = 2048
NH = 32
DL = 512
NI = 86
NDC = 32
EPS = 1e-5
ISQ = 1.0 / math.sqrt(128.0)
TWO_PI = 2.0 * math.pi
LN1E4 = math.log(10000.0)

SBUF_BYTES = 229376
WIN_BASE = 40960


class Win:

    def __init__(self, nc):
        self.nc = nc
        self.off = WIN_BASE

    def alloc(self, name, shape, dtype):
        esz = 2 if dtype == BF16 else 4
        nbytes = int(np.prod(shape[1:])) * esz
        off = (self.off + 31) // 32 * 32
        h = self.nc.alloc_sbuf_tensor_at(name, shape, dtype, offset=off)
        self.off = off + nbytes
        assert self.off <= SBUF_BYTES, f"SBUF overflow at {name}: {self.off}"
        return h

    def reset(self, off=WIN_BASE):
        self.off = off


def build(phase_limit=99, debug=False, norope=False):
    nc = bass.Bass("TRN2", target_bir_lowering=False)

    x_own = nc.dram_tensor("x_own", [T, D], F32, kind="ExternalInput").ap()
    x_oth = nc.dram_tensor("x_oth", [T, D], F32, kind="ExternalInput").ap()
    wq = nc.dram_tensor("wq", [D, D], F32, kind="ExternalInput").ap()
    wc = nc.dram_tensor("wc", [D, DL], F32, kind="ExternalInput").ap()
    wk = nc.dram_tensor("wk", [DL, D], F32, kind="ExternalInput").ap()
    wv = nc.dram_tensor("wv", [DL, D], F32, kind="ExternalInput").ap()
    wo = nc.dram_tensor("wo", [D, D], F32, kind="ExternalInput").ap()
    gate_t = nc.dram_tensor("gate_t", [D, NI * 128], I32, kind="ExternalInput").ap()
    up_t = nc.dram_tensor("up_t", [D, NI * 128], I32, kind="ExternalInput").ap()
    down_t = nc.dram_tensor("down_t", [NI * 128, D], I32, kind="ExternalInput").ap()
    gsc = nc.dram_tensor("gsc", [1, NI * 128], F32, kind="ExternalInput").ap()
    usc = nc.dram_tensor("usc", [1, NI * 128], F32, kind="ExternalInput").ap()
    dsc = nc.dram_tensor("dsc", [1, D], F32, kind="ExternalInput").ap()

    out = nc.dram_tensor("out", [T, D], F32, kind="ExternalOutput").ap()

    def scratch(name, shape, dtype):
        kind = "ExternalOutput" if debug else "Internal"
        return nc.dram_tensor(name, shape, dtype, kind=kind).ap()

    XN = scratch("XN", [D, S], BF16)
    QR = scratch("QR", [D, T], BF16)
    AU = scratch("AU", [D, T], BF16)
    ATT = scratch("ATT", [T, D], F32)
    GU = scratch("GU", [T // 256, NI, 128, 256], BF16)
    CS = scratch("CS", [2, 128, S], BF16) if debug else None
    MVD = scratch("MVD", [128, 3 * 32], F32) if debug else None
    PB = nc.dram_tensor("PB", [1, 1], I32)

    es = contextlib.ExitStack()
    with es:
        E = es.enter_context

        ident_f = E(nc.sbuf_tensor("ident_f", [128, 128], F32))
        ones_cb = E(nc.sbuf_tensor("ones_cb", [128, 1], BF16))
        ones_rf = E(nc.sbuf_tensor("ones_rf", [1, 128], F32))
        b0_i = E(nc.sbuf_tensor("b0_i", [1, 1], I32))
        b0_f = E(nc.sbuf_tensor("b0_f", [1, 1], F32))
        cos_t = E(nc.sbuf_tensor("cos_t", [128, S], BF16))
        sinf_t = E(nc.sbuf_tensor("sinf_t", [128, S], BF16))
        qmax_h = E(nc.sbuf_tensor("qmax_h", [1, NH], F32))
        halfpi_t = E(nc.sbuf_tensor("halfpi_t", [128, 1], F32))
        eps_t = E(nc.sbuf_tensor("eps_t", [128, 1], F32))

        used = SBUF_BYTES - nc.sbuf_bytes_remaining
        assert used <= WIN_BASE, f"persistent SBUF {used} exceeds window base {WIN_BASE}"

        PA = E(nc.psum_tensor("PA", [128, 1024], F32))
        PBk = E(nc.psum_tensor("PBk", [128, 1024], F32))
        PC = E(nc.psum_tensor("PC", [128, 1024], F32))
        SM = E(nc.psum_tensor("SM", [1, 1024], F32))
        PS6 = [PA[:, 0:512], PA[:, 512:1024], PBk[:, 0:512], PBk[:, 512:1024],
               PC[:, 0:512], PC[:, 512:1024]]

        sem_names = [
            "bar",
            "g0", "t0", "v0", "a0",
            "l1", "v1", "a1", "t1", "ve1", "st1",
            "l2", "wl2", "wf2", "t2", "a2", "v2", "a2b", "t2b", "v2b", "st2",
            "l3", "wl3", "t3", "v3",
            "wl4", "ql4", "t4", "a4", "t4b", "a4b", "v4", "a4c", "t4c",
            "t5", "a5", "t5b", "v5b", "t5c", "v5", "st5", "a5r",
            "l6", "wl6", "t6", "v6", "st6",
            "l7", "wl7", "wf7", "t7", "a7", "v7", "st7",
            "l8", "l8d", "wl8", "gl8", "gl8b", "t8", "v8", "st8", "t8d", "v8d",
            "st1a", "st1b", "st2a", "st2b", "wl2a", "wl2b", "wl4a", "wl4b",
            "ql4a", "ql4b", "st5a", "st5b", "l6a", "l6xa", "l6xb", "wl6a",
            "wl6b", "st6a", "st6b", "wl7a", "wl7b", "st7a", "st7b", "wl8a",
            "wl8b", "gl8c", "gl8d", "l8a", "l8b", "st8a", "st8b",
        ]
        sems = {n: E(nc.semaphore(n)) for n in sem_names}
        sm = lambda n: sems[n]
        psem = lambda base, i: sems[base + ("a" if i % 2 == 0 else "b")]

        win = Win(nc)

        win.reset()
        iota_r = win.alloc("iota_r", [1, S], F32)
        pos_r = win.alloc("pos_r", [1, S], F32)
        dmod_r = win.alloc("dmod_r", [1, 128], F32)
        invf_r = win.alloc("invf_r", [1, 128], F32)
        p0_tq = win.alloc("p0_tq", [128, 512], F32)
        p0_tqc = win.alloc("p0_tqc", [128, 512], F32)
        p0_ti = win.alloc("p0_ti", [128, 512], I32)
        p0_tf = win.alloc("p0_tf", [128, 512], F32)
        p0_ci = win.alloc("p0_ci", [128, 512], I32)
        p0_cf = win.alloc("p0_cf", [128, 512], F32)
        p0_red = win.alloc("p0_red", [128, 512], F32)
        p0_redc = win.alloc("p0_redc", [128, 512], F32)
        p0_sin = win.alloc("p0_sin", [128, 512], F32)
        p0_ones = win.alloc("p0_ones", [128, 128], F32)

        win.reset()
        x_b = [win.alloc(f"x_b{i}", [128, D], F32) for i in range(2)]
        xnf_b = [win.alloc(f"xnf_b{i}", [128, D], F32) for i in range(2)]
        bns_b = win.alloc("bns_b", [128, 48], F32)
        mv_b = win.alloc("mv_b", [128, 2], F32)
        sq_b = win.alloc("sq_b", [128, 1], F32)
        rstd_b = win.alloc("rstd_b", [128, 1], F32)
        ev_b = [win.alloc(f"ev_b{i}", [128, 512], BF16) for i in range(2)]
        mvd_sb = win.alloc("mvd_sb", [128, 96], F32) if debug else None

        win.reset()
        xnq = win.alloc("xnq", [128, NDC * T], BF16)
        wqc = [win.alloc(f"wqc{i}", [128, NDC * 128], BF16) for i in range(2)]
        qev_b = [win.alloc(f"qev_b{i}", [128, 512], BF16) for i in range(2)]
        shq_b = win.alloc("shq_b", [128, 512], BF16)
        t1_b = win.alloc("t1_b", [128, 512], BF16)
        t2_b = win.alloc("t2_b", [128, 512], BF16)
        qr_b = [win.alloc(f"qr_b{i}", [128, 512], BF16) for i in range(2)]
        qsq_b = win.alloc("qsq_b", [128, 512], BF16)
        qmx_b = win.alloc("qmx_b", [1, 128], F32)

        win.reset()
        lat = win.alloc("lat", [128, 4 * S], BF16)
        p3_base = win.off
        wc_sb = win.alloc("wc_sb", [128, NDC * 512], BF16)
        xnl_b = [win.alloc(f"xnl_b{i}", [128, NDC * 512], BF16) for i in range(2)]

        win.reset(p3_base)
        wkc = [win.alloc(f"wkc{i}", [128, 4 * 128], BF16) for i in range(2)]
        wvc = [win.alloc(f"wvc{i}", [128, 4 * 128], BF16) for i in range(2)]
        qrh = [win.alloc(f"qrh{i}", [128, T], BF16) for i in range(2)]
        k_sb = win.alloc("k_sb", [128, S], BF16)
        shk_sb = win.alloc("shk_sb", [128, S], BF16)
        kt1_sb = win.alloc("kt1_sb", [128, S], BF16)
        kr_sb = win.alloc("kr_sb", [128, S], BF16)
        v_sb = win.alloc("v_sb", [128, S], BF16)
        ksq_sb = win.alloc("ksq_sb", [128, 512], BF16)
        kms_sb = win.alloc("kms_sb", [1, 8], F32)
        kmax_sb = win.alloc("kmax_sb", [1, 1], F32)
        msq_sb = win.alloc("msq_sb", [1, 1], F32)
        m_sb = win.alloc("m_sb", [1, 1], F32)
        lnm_sb = win.alloc("lnm_sb", [1, 1], F32)
        bias_sb = win.alloc("bias_sb", [128, 1], F32)
        e_sb = [win.alloc(f"e_sb{i}", [128, 1024], BF16) for i in range(2)]
        recip_sb = win.alloc("recip_sb", [1, 1024], F32)
        rb_sb = win.alloc("rb_sb", [128, 1024], BF16)
        au_b = [win.alloc(f"au_b{i}", [128, T], BF16) for i in range(2)]

        win.reset()
        au_res = win.alloc("au_res", [128, NDC * 1024], BF16)
        woc = [win.alloc(f"woc{i}", [128, NDC * 512], BF16) for i in range(2)]
        x4_b = [win.alloc(f"x4_b{i}", [128, 512], F32) for i in range(2)]
        at4_b = [win.alloc(f"at4_b{i}", [128, 512], F32) for i in range(2)]

        win.reset()
        xn5 = win.alloc("xn5", [128, NDC * T], BF16)
        wg5 = [win.alloc(f"wg5{i}", [128, NDC * 128], BF16) for i in range(2)]
        wu5 = [win.alloc(f"wu5{i}", [128, NDC * 128], BF16) for i in range(2)]
        u5_b = [win.alloc(f"u5_b{i}", [128, 512], F32) for i in range(2)]
        gu5_b = [win.alloc(f"gu5_b{i}", [128, 512], BF16) for i in range(4)]
        gsc_sb = win.alloc("gsc_sb", [128, NI], F32)
        usc_sb = win.alloc("usc_sb", [128, NI], F32)
        sc5_sb = win.alloc("sc5_sb", [128, NI], F32)

        win.reset()
        dcol = [win.alloc(f"dcol{i}", [128, NI * 256], BF16) for i in range(2)]
        gub = [win.alloc(f"gub{i}", [128, NI * 256], BF16) for i in range(2)]
        dscb_sl = win.alloc("dscb_sl", [128, 256], F32)
        at5_b = [win.alloc(f"at5_b{i}", [128, 256], F32) for i in range(2)]
        o5_b = [win.alloc(f"o5_b{i}", [128, 256], F32) for i in range(2)]

        block = E(nc.Block())

        def barrier(eng, idx):
            eng.nop().then_inc(sm("bar"), 1)
            eng.wait_ge(sm("bar"), 5 * idx)

        def xn_view(dc, t0, tn):
            return XN.rearrange("(a p) m -> p a m", p=128)[:, dc, t0:t0 + tn]

        def p0_g(g):
            pid = g.partition_id()
            with g.register("r0") as r0:
                g.reg_mov(r0, (pid % 2) * 2048)
                g.reg_save(PB.ap()[0:1, 0:1], r0)
            g.nop().then_inc(sm("g0"), 1)
            g.wait_ge(sm("g0"), 1)
            g.dma_start(b0_i[:], PB.ap()).then_inc(sm("g0"), 16)
            g.memset(halfpi_t[:], math.pi / 2).then_inc(sm("g0"), 1)
            g.memset(eps_t[:], EPS).then_inc(sm("g0"), 1)
            g.iota(iota_r[:], pattern=[[1, S]], base=0, channel_multiplier=0,
                   allow_small_or_imprecise_dtypes=True).then_inc(sm("g0"), 1)
            g.iota(dmod_r[:], pattern=[[0, 2], [1, 64]], base=0, channel_multiplier=0,
                   allow_small_or_imprecise_dtypes=True).then_inc(sm("g0"), 1)
            g.memset(p0_ones[:], 1.0).then_inc(sm("g0"), 1)
            g.wait_ge(sm("g0"), 22)
            g.affine_select(ident_f[:], p0_ones[:], pattern=[[1, 128]],
                            compare_op=ALU.is_equal, fill=0.0,
                            base=0, channel_multiplier=-1).then_inc(sm("g0"), 1)
            g.memset(ones_cb[:], 1.0).then_inc(sm("g0"), 1)
            g.memset(ones_rf[:], 1.0).then_inc(sm("g0"), 1)
            g.wait_ge(sm("g0"), 25)

        def p0_v(v):
            v.wait_ge(sm("g0"), 20)
            v.tensor_copy(b0_f[:], b0_i[:]).then_inc(sm("v0"), 1)
            v.wait_ge(sm("v0"), 1)
            v.tensor_scalar(pos_r[0:1, 0:T], iota_r[0:1, 0:T], b0_f[0:1, 0:1], None,
                            op0=ALU.add).then_inc(sm("v0"), 1)
            v.tensor_scalar(pos_r[0:1, T:S], iota_r[0:1, T:S], b0_f[0:1, 0:1], None,
                            op0=ALU.subtract).then_inc(sm("v0"), 1)
            for pt in range(8):
                v.wait_ge(sm("t0"), pt + 1)
                v.tensor_scalar(p0_tq[:], PA[:, 0:512], 1.0 / TWO_PI, None,
                                op0=ALU.mult).then_inc(sm("v0"), 1)
                v.tensor_scalar(p0_tqc[:], PA[:, 0:512], 1.0 / TWO_PI, 0.25,
                                op0=ALU.mult, op1=ALU.add).then_inc(sm("v0"), 1)
                v.tensor_copy(p0_ti[:], p0_tq[:]).then_inc(sm("v0"), 1)
                v.tensor_copy(p0_tf[:], p0_ti[:]).then_inc(sm("v0"), 1)
                v.tensor_copy(p0_ci[:], p0_tqc[:]).then_inc(sm("v0"), 1)
                v.tensor_copy(p0_cf[:], p0_ci[:]).then_inc(sm("v0"), 1)
                v.scalar_tensor_tensor(p0_red[:], p0_tf[:], -TWO_PI, PA[:, 0:512],
                                       op0=ALU.mult, op1=ALU.add).then_inc(sm("v0"), 1)
                v.scalar_tensor_tensor(p0_redc[:], p0_cf[:], -TWO_PI, PA[:, 0:512],
                                       op0=ALU.mult, op1=ALU.add).then_inc(sm("v0"), 1)
                v.wait_ge(sm("a0"), 3 + pt * 2)
                sl = slice(pt * 512, (pt + 1) * 512)
                v.tensor_copy(sinf_t[64:128, sl], p0_sin[64:128, :]).then_inc(sm("v0"), 1)
                v.tensor_scalar(sinf_t[0:64, sl], p0_sin[0:64, :], -1.0, None,
                                op0=ALU.mult).then_inc(sm("v0"), 1)
            if True:
                v.wait_ge(sm("v0"), 83)

        def p0_s(s):
            s.wait_ge(sm("g0"), 21)
            s.activation(invf_r[:], dmod_r[:], AF.Exp,
                         scale=-LN1E4 / 64.0).then_inc(sm("a0"), 1)
            for pt in range(8):
                base = 3 + pt * 10
                sl = slice(pt * 512, (pt + 1) * 512)
                s.wait_ge(sm("v0"), base + 7)
                s.activation(p0_sin[:], p0_red[:], AF.Sin).then_inc(sm("a0"), 1)
                s.wait_ge(sm("v0"), base + 8)
                s.activation(cos_t[:, sl], p0_redc[:], AF.Sin,
                             bias=halfpi_t[:]).then_inc(sm("a0"), 1)
            if True:
                s.wait_ge(sm("a0"), 17)

        def p0_t(t):
            t.wait_ge(sm("a0"), 1)
            t.wait_ge(sm("v0"), 3)
            for pt in range(8):
                if pt > 0:
                    t.wait_ge(sm("v0"), 3 + (pt - 1) * 10 + 8)
                t.matmul(PA[:, 0:512], invf_r[:], pos_r[0:1, pt * 512:(pt + 1) * 512],
                         start=True, stop=True).then_inc(sm("t0"), 1)

        def p1_sy(sy):
            dbg_st = 0
            if debug:
                sy.dma_start(CS[0], cos_t[:]).then_inc(sm("st1"), 16)
                sy.dma_start(CS[1], sinf_t[:]).then_inc(sm("st1"), 16)
                dbg_st = 2
            for pt in range(32):
                src = x_own if pt < 16 else x_oth
                r0 = (pt % 16) * 128
                if pt >= 1:
                    sy.wait_ge(sm("v1"), 4 * (pt - 1) + 4)
                sy.dma_start(x_b[pt % 2][:], src[r0:r0 + 128, :]).then_inc(sm("l1"), 16)
                for dcg in range(8):
                    n = pt * 8 + dcg
                    sy.wait_ge(sm("ve1"), n + 1)
                    sy.dma_start(
                        XN.rearrange("(a p) m -> p a m", p=128)
                        [:, dcg * 4:(dcg + 1) * 4, pt * 128:(pt + 1) * 128],
                        ev_b[dcg % 2].ap().rearrange("p (a m) -> p a m", a=4),
                    ).then_inc(psem("st1", n), 16)
            if debug:
                sy.wait_ge(sm("v1"), 128)
                sy.dma_start(MVD, mvd_sb[:]).then_inc(sm("st1"), 16)
                dbg_st += 1
            sy.wait_ge(psem("st1", 0), 16 * 128)
            sy.wait_ge(psem("st1", 1), 16 * 128)
            if dbg_st:
                sy.wait_ge(sm("st1"), 16 * dbg_st)

        def p1_v(v):
            for pt in range(32):
                v.wait_ge(sm("l1"), 16 * (pt + 1))
                for j in range(8):
                    mi = v.bn_stats(bns_b[:, j * 6:(j + 1) * 6],
                                    x_b[pt % 2][:, j * 512:(j + 1) * 512])
                mi.then_inc(sm("v1"), 1)
                v.wait_ge(sm("v1"), 4 * pt + 1)
                v.bn_aggr(mv_b[:], bns_b[:]).then_inc(sm("v1"), 1)
                v.wait_ge(sm("a1"), pt + 1)
                v.reciprocal(rstd_b[:], sq_b[:]).then_inc(sm("v1"), 1)
                v.wait_ge(sm("v1"), 4 * pt + 3)
                if pt >= 2:
                    v.wait_ge(sm("t1"), 8 * (pt - 2) + 8)
                v.tensor_scalar(xnf_b[pt % 2][:], x_b[pt % 2][:], mv_b[:, 0:1],
                                rstd_b[:], op0=ALU.subtract,
                                op1=ALU.mult).then_inc(sm("v1"), 1)
                if debug:
                    v.tensor_copy(mvd_sb[:, 3 * pt:3 * pt + 2], mv_b[:])
                    v.tensor_copy(mvd_sb[:, 3 * pt + 2:3 * pt + 3], rstd_b[:])
            v.wait_ge(sm("v1"), 128)

        def p1_s(s):
            for pt in range(32):
                s.wait_ge(sm("v1"), 4 * pt + 2)
                s.activation(sq_b[:], mv_b[:, 1:2], AF.Sqrt,
                             bias=eps_t[:]).then_inc(sm("a1"), 1)
                for dcg in range(8):
                    n = pt * 8 + dcg
                    s.wait_ge(sm("t1"), n + 1)
                    if n >= 2:
                        s.wait_ge(psem("st1", n), 16 * (n // 2))
                    s.activation(ev_b[dcg % 2][:],
                                 (PA if dcg % 2 == 0 else PBk)[:, 0:512],
                                 AF.Copy).then_inc(sm("ve1"), 1)
            s.wait_ge(sm("a1"), 32)
            s.wait_ge(sm("ve1"), 256)

        def p1_t(t):
            for pt in range(32):
                t.wait_ge(sm("v1"), 4 * pt + 4)
                for dcg in range(8):
                    n = pt * 8 + dcg
                    ps = (PA if dcg % 2 == 0 else PBk)
                    if n >= 2:
                        t.wait_ge(sm("ve1"), n - 1)
                    for j in range(4):
                        dc = dcg * 4 + j
                        mi = t.transpose(ps[:, j * 128:(j + 1) * 128],
                                         xnf_b[pt % 2][:, dc * 128:(dc + 1) * 128],
                                         ident_f[:])
                    mi.then_inc(sm("t1"), 1)

        def p2q_sy(sy):
            for j in range(4):
                sy.dma_start(
                    xnq.ap().rearrange("p (a m) -> p a m", a=NDC)[:, j * 8:(j + 1) * 8, :],
                    xn_view(slice(j * 8, (j + 1) * 8), 0, T)).then_inc(sm("l2"), 16)
            for n in range(128):
                fc, qt = divmod(n, 4)
                sy.wait_ge(sm("v2"), 5 * n + 5)
                sy.dma_start(QR[fc * 128:(fc + 1) * 128, qt * 512:(qt + 1) * 512],
                             qr_b[n % 2][:]).then_inc(psem("st2", n), 16)
            sy.wait_ge(psem("st2", 0), 16 * 64)
            sy.wait_ge(psem("st2", 1), 16 * 64)

        def p2q_g(g):
            for fc in range(NH):
                if fc >= 2:
                    g.wait_ge(sm("wf2"), fc - 1)
                g.dma_start(wqc[fc % 2][:],
                            wq.rearrange("(a p) m -> p a m", p=128)
                            [:, :, fc * 128:(fc + 1) * 128]).then_inc(psem("wl2", fc), 16)

        def p2q_t(t):
            t.wait_ge(sm("l2"), 64)
            for n in range(128):
                fc, qt = divmod(n, 4)
                slot = PS6[n % 6]
                if qt == 0:
                    t.wait_ge(psem("wl2", fc), 16 * (fc // 2 + 1))
                if n >= 6:
                    t.wait_ge(sm("a2"), n - 5)
                for dc in range(NDC):
                    mi = t.matmul(slot, wqc[fc % 2][:, dc * 128:(dc + 1) * 128],
                                  xnq[:, dc * T + qt * 512: dc * T + qt * 512 + 512],
                                  start=(dc == 0), stop=(dc == NDC - 1))
                mi.then_inc(sm("t2"), 1)
                if qt == 3:
                    t.nop().then_inc(sm("wf2"), 1)
                if n >= 1:
                    t.wait_ge(sm("a2b"), n)
                    if n >= 2:
                        t.wait_ge(sm("v2b"), n - 1)
                    t.matmul(SM[0:1, 0:512], ones_cb[:], qsq_b[:],
                             start=True, stop=True).then_inc(sm("t2b"), 1)
            t.wait_ge(sm("a2b"), 128)
            t.wait_ge(sm("v2b"), 127)
            t.matmul(SM[0:1, 0:512], ones_cb[:], qsq_b[:],
                     start=True, stop=True).then_inc(sm("t2b"), 1)

        def p2q_s(s):
            for n in range(128):
                slot = PS6[n % 6]
                s.wait_ge(sm("t2"), n + 1)
                s.activation(qev_b[n % 2][:], slot, AF.Copy).then_inc(sm("a2"), 1)
                s.wait_ge(sm("v2"), 5 * n + 5)
                if n >= 1:
                    s.wait_ge(sm("t2b"), n)
                s.activation(qsq_b[:], qr_b[n % 2][:], AF.Square).then_inc(sm("a2b"), 1)
            s.wait_ge(sm("a2"), 128)
            s.wait_ge(sm("a2b"), 128)

        def p2q_v(v):
            for n in range(128):
                fc, qt = divmod(n, 4)
                sl = slice(qt * 512, (qt + 1) * 512)
                v.wait_ge(sm("a2"), n + 1)
                v.tensor_copy(shq_b[0:64, :], qev_b[n % 2][64:128, :]).then_inc(sm("v2"), 1)
                v.tensor_copy(shq_b[64:128, :], qev_b[n % 2][0:64, :]).then_inc(sm("v2"), 1)
                v.tensor_tensor(t1_b[:], qev_b[n % 2][:], cos_t[:, sl],
                                op=ALU.mult).then_inc(sm("v2"), 1)
                v.tensor_tensor(t2_b[:], shq_b[:], sinf_t[:, sl],
                                op=ALU.mult).then_inc(sm("v2"), 1)
                if n >= 2:
                    v.wait_ge(psem("st2", n), 16 * (n // 2))
                    v.wait_ge(sm("a2b"), n - 1)
                if norope:
                    v.tensor_copy(qr_b[n % 2][:], qev_b[n % 2][:]).then_inc(sm("v2"), 1)
                else:
                    v.tensor_tensor(qr_b[n % 2][:], t1_b[:], t2_b[:],
                                    op=ALU.add).then_inc(sm("v2"), 1)
                if n >= 1:
                    v.wait_ge(sm("t2b"), n)
                    v.reduce_max(qmx_b[0:1, n - 1:n], SM[0:1, 0:512],
                                 axis=AX.X).then_inc(sm("v2b"), 1)
            v.wait_ge(sm("t2b"), 128)
            v.reduce_max(qmx_b[0:1, 127:128], SM[0:1, 0:512],
                         axis=AX.X).then_inc(sm("v2b"), 1)
            v.wait_ge(sm("v2b"), 128)
            for fc in range(NH):
                v.reduce_max(qmax_h[0:1, fc:fc + 1], qmx_b[0:1, fc * 4:(fc + 1) * 4],
                             axis=AX.X).then_inc(sm("v2b"), 1)
            v.wait_ge(sm("v2b"), 160)
            v.wait_ge(sm("v2"), 640)

        def p2l_sy(sy):
            for pt in range(8):
                if pt >= 1:
                    sy.wait_ge(sm("t3"), 4 * pt)
                for j in range(4):
                    sy.dma_start(
                        xnl_b[pt % 2].ap().rearrange("p (a m) -> p a m", a=NDC)
                        [:, j * 8:(j + 1) * 8, :],
                        xn_view(slice(j * 8, (j + 1) * 8), pt * 512, 512)).then_inc(
                            sm("l3"), 16)

        def p2l_g(g):
            g.dma_start(wc_sb[:],
                        wc.rearrange("(a p) m -> p a m", p=128)).then_inc(sm("wl3"), 16)

        def p2l_t(t):
            t.wait_ge(sm("wl3"), 16)
            for pt in range(8):
                t.wait_ge(sm("l3"), 16 * 4 * (pt + 1))
                for lc in range(4):
                    n = pt * 4 + lc
                    slot = PS6[n % 6]
                    if n >= 6:
                        t.wait_ge(sm("v3"), n - 5)
                    for dc in range(NDC):
                        mi = t.matmul(slot,
                                      wc_sb[:, dc * 512 + lc * 128: dc * 512 + lc * 128 + 128],
                                      xnl_b[pt % 2][:, dc * 512:(dc + 1) * 512],
                                      start=(dc == 0), stop=(dc == NDC - 1))
                    mi.then_inc(sm("t3"), 1)

        def p2l_v(v):
            for n in range(32):
                pt, lc = divmod(n, 4)
                slot = PS6[n % 6]
                v.wait_ge(sm("t3"), n + 1)
                v.tensor_copy(lat[:, lc * S + pt * 512: lc * S + pt * 512 + 512],
                              slot).then_inc(sm("v3"), 1)
            v.wait_ge(sm("v3"), 32)

        T4H, A4H, T4BH, A4BH = 8, 8, 8, 8
        V4H = 16
        A4CH = 10
        T4CH = 9
        T5H, A5H, T5BH = 64, 64, 64
        V5BH, T5CH, V5H = 2, 2, 2
        A5RH = 2

        def p3_g(g):
            for h in range(NH):
                if h >= 2:
                    g.wait_ge(sm("t4"), T4H * (h - 1))
                    g.wait_ge(sm("t4b"), T4BH * (h - 1))
                g.dma_start(wkc[h % 2][:],
                            wk.rearrange("(a p) m -> p a m", p=128)
                            [:, :, h * 128:(h + 1) * 128]).then_inc(psem("wl4", h), 16)
                g.dma_start(wvc[h % 2][:],
                            wv.rearrange("(a p) m -> p a m", p=128)
                            [:, :, h * 128:(h + 1) * 128]).then_inc(psem("wl4", h), 16)

        def p3_sy(sy):
            for h in range(NH):
                if h >= 2:
                    sy.wait_ge(sm("t5"), T5H * (h - 1))
                sy.dma_start(qrh[h % 2][:],
                             QR[h * 128:(h + 1) * 128, :]).then_inc(psem("ql4", h), 16)
                sy.wait_ge(sm("v5"), V5H * h + 2)
                sy.dma_start(AU[h * 128:(h + 1) * 128, :],
                             au_b[h % 2][:]).then_inc(psem("st5", h), 16)
            sy.wait_ge(psem("st5", 0), 16 * 16)
            sy.wait_ge(psem("st5", 1), 16 * 16)

        def p3_t(t):
            for h in range(NH):
                t.wait_ge(psem("wl4", h), 32 * (h // 2 + 1))
                t.wait_ge(psem("ql4", h), 16 * (h // 2 + 1))
                if h >= 1:
                    t.wait_ge(sm("v5"), V5H * h)
                for pt in range(8):
                    slot = PS6[pt % 4]
                    if pt >= 4:
                        t.wait_ge(sm("a4"), A4H * h + pt - 3)
                    for lc in range(4):
                        mi = t.matmul(slot, wkc[h % 2][:, lc * 128:(lc + 1) * 128],
                                      lat[:, lc * S + pt * 512: lc * S + pt * 512 + 512],
                                      start=(lc == 0), stop=(lc == 3))
                    mi.then_inc(sm("t4"), 1)
                for grp in range(8):
                    slot = PS6[4 + grp % 2]
                    if grp >= 2:
                        t.wait_ge(sm("a4b"), A4BH * h + grp - 1)
                    for j in range(4):
                        pc = grp * 4 + j
                        for lc in range(4):
                            mi = t.matmul(
                                slot[:, j * 128:(j + 1) * 128],
                                lat[:, lc * S + pc * 128: lc * S + pc * 128 + 128],
                                wvc[h % 2][:, lc * 128:(lc + 1) * 128],
                                start=(lc == 0), stop=(lc == 3))
                    mi.then_inc(sm("t4b"), 1)
                for j in range(8):
                    t.wait_ge(sm("a4c"), A4CH * h + j + 1)
                    if h >= 1 and j == 0:
                        t.wait_ge(sm("v5b"), V5BH * h)
                    if j >= 1:
                        t.wait_ge(sm("v4"), V4H * h + 5 + j)
                    t.matmul(SM[0:1, 0:512], ones_cb[:], ksq_sb[:],
                             start=True, stop=True).then_inc(sm("t4c"), 1)
                t.wait_ge(sm("a4c"), A4CH * h + 10)
                t.wait_ge(sm("a4"), A4H * h + 8)
                t.matmul(PA[:, 0:1], ones_rf[:], m_sb[0:1, 0:1],
                         start=True, stop=True).then_inc(sm("t4c"), 1)
                for qh in range(2):
                    q0 = qh * 1024
                    for kc in range(33):
                        if kc < 32:
                            sg = PBk if kc % 2 == 0 else PC
                            if kc == 0:
                                t.wait_ge(sm("v4"), V4H * (h + 1))
                                t.wait_ge(sm("a4b"), A4BH * h + 8)
                                if qh == 0:
                                    t.wait_ge(sm("v5"), V5H * h)
                                else:
                                    t.wait_ge(sm("v5"), V5H * h + 1)
                                    t.wait_ge(sm("a5r"), A5RH * h + 1)
                                t.wait_ge(sm("v5b"), V5BH * h + qh)
                            if kc >= 2:
                                t.wait_ge(sm("a5"), A5H * h + 32 * qh + kc - 1)
                            for half in range(2):
                                mi = t.matmul(sg[:, half * 512:(half + 1) * 512],
                                              kr_sb[:, kc * 128:(kc + 1) * 128],
                                              qrh[h % 2][:, q0 + half * 512: q0 + half * 512 + 512],
                                              start=True, stop=True)
                            mi.then_inc(sm("t5"), 1)
                        if kc >= 1:
                            kd = kc - 1
                            t.wait_ge(sm("a5"), A5H * h + 32 * qh + kd + 1)
                            for half in range(2):
                                t.matmul(PA[:, half * 512:(half + 1) * 512],
                                         v_sb[:, kd * 128:(kd + 1) * 128],
                                         e_sb[kd % 2][:, half * 512:(half + 1) * 512],
                                         start=(kd == 0), stop=(kd == 31))
                            for half in range(2):
                                mi = t.matmul(SM[0:1, half * 512:(half + 1) * 512],
                                              ones_cb[:],
                                              e_sb[kd % 2][:, half * 512:(half + 1) * 512],
                                              start=(kd == 0), stop=(kd == 31))
                            mi.then_inc(sm("t5b"), 1)
                    t.wait_ge(sm("v5b"), V5BH * h + qh + 1)
                    for half in range(2):
                        mi = t.matmul(PBk[:, half * 512:(half + 1) * 512], ones_rf[:],
                                      recip_sb[0:1, half * 512:(half + 1) * 512],
                                      start=True, stop=True)
                    mi.then_inc(sm("t5c"), 1)

        def p3_s(s):
            for h in range(NH):
                for pt in range(8):
                    slot = PS6[pt % 4]
                    s.wait_ge(sm("t4"), T4H * h + pt + 1)
                    s.activation(k_sb[:, pt * 512:(pt + 1) * 512], slot,
                                 AF.Copy).then_inc(sm("a4"), 1)
                for grp in range(8):
                    slot = PS6[4 + grp % 2]
                    s.wait_ge(sm("t4b"), T4BH * h + grp + 1)
                    s.activation(v_sb[:, grp * 512:(grp + 1) * 512], slot,
                                 AF.Copy).then_inc(sm("a4b"), 1)
                for j in range(8):
                    s.wait_ge(sm("a4"), A4H * h + j + 1)
                    if j >= 1:
                        s.wait_ge(sm("t4c"), T4CH * h + j)
                    s.activation(ksq_sb[:], k_sb[:, j * 512:(j + 1) * 512],
                                 AF.Square).then_inc(sm("a4c"), 1)
                s.wait_ge(sm("v4"), V4H * h + 15)
                s.activation(lnm_sb[:], msq_sb[:], AF.Ln).then_inc(sm("a4c"), 1)
                s.activation(m_sb[:], lnm_sb[:], AF.Exp, scale=0.5).then_inc(sm("a4c"), 1)
                for qh in range(2):
                    s.wait_ge(sm("v4"), V4H * (h + 1))
                    for kc in range(32):
                        sg = PBk if kc % 2 == 0 else PC
                        s.wait_ge(sm("t5"), T5H * h + 32 * qh + kc + 1)
                        if kc >= 2:
                            s.wait_ge(sm("t5b"), T5BH * h + 32 * qh + kc - 1)
                        s.activation(e_sb[kc % 2][:], sg[:, 0:1024], AF.Exp,
                                     bias=bias_sb[:],
                                     scale=ISQ).then_inc(sm("a5"), 1)
                    s.wait_ge(sm("t5b"), T5BH * h + 32 * (qh + 1))
                    s.activation(recip_sb[:], SM[0:1, :], AF.Ln).then_inc(sm("v5b"), 1)
                    s.wait_ge(sm("t5c"), T5CH * h + qh + 1)
                    s.activation(rb_sb[:], PBk[:, 0:1024], AF.Exp,
                                 scale=-1.0).then_inc(sm("a5r"), 1)
            s.wait_ge(sm("a5"), 2048)
            s.wait_ge(sm("a5r"), A5RH * NH)

        def p3_v(v):
            for h in range(NH):
                v.wait_ge(sm("a4"), A4H * h + 8)
                v.tensor_copy(shk_sb[0:64, :], k_sb[64:128, :]).then_inc(sm("v4"), 1)
                v.tensor_copy(shk_sb[64:128, :], k_sb[0:64, :]).then_inc(sm("v4"), 1)
                v.tensor_tensor(kt1_sb[:], k_sb[:], cos_t[:],
                                op=ALU.mult).then_inc(sm("v4"), 1)
                v.tensor_tensor(shk_sb[:], shk_sb[:], sinf_t[:],
                                op=ALU.mult).then_inc(sm("v4"), 1)
                v.tensor_tensor(kr_sb[:], kt1_sb[:], shk_sb[:],
                                op=ALU.add).then_inc(sm("v4"), 1)
                for j in range(8):
                    v.wait_ge(sm("t4c"), T4CH * h + j + 1)
                    v.reduce_max(kms_sb[0:1, j:j + 1], SM[0:1, 0:512],
                                 axis=AX.X).then_inc(sm("v4"), 1)
                v.wait_ge(sm("v4"), V4H * h + 13)
                v.reduce_max(kmax_sb[:], kms_sb[:], axis=AX.X).then_inc(sm("v4"), 1)
                v.wait_ge(sm("v4"), V4H * h + 14)
                v.tensor_tensor(msq_sb[:], kmax_sb[:], qmax_h[0:1, h:h + 1],
                                op=ALU.mult).then_inc(sm("v4"), 1)
                v.wait_ge(sm("t4c"), T4CH * h + 9)
                v.tensor_scalar(bias_sb[:], PA[:, 0:1], -ISQ, None,
                                op0=ALU.mult).then_inc(sm("v4"), 1)
                for qh in range(2):
                    v.wait_ge(sm("a5r"), A5RH * h + qh + 1)
                    v.wait_ge(sm("t5b"), T5BH * h + 32 * (qh + 1))
                    if h >= 2 and qh == 0:
                        v.wait_ge(psem("st5", h), 16 * (h // 2))
                    v.tensor_tensor(au_b[h % 2][:, qh * 1024:(qh + 1) * 1024],
                                    PA[:, 0:1024], rb_sb[:],
                                    op=ALU.mult).then_inc(sm("v5"), 1)
            v.wait_ge(sm("v5"), V5H * NH)
            v.wait_ge(sm("v4"), 512)

        def p4_sy(sy):
            for th in range(2):
                if th == 1:
                    sy.wait_ge(sm("t6"), 64)
                for j in range(4):
                    sy.dma_start(
                        au_res.ap().rearrange("p (a m) -> p a m", a=NDC)
                        [:, j * 8:(j + 1) * 8, :],
                        AU.rearrange("(a p) m -> p a m", p=128)
                        [:, j * 8:(j + 1) * 8, th * 1024:(th + 1) * 1024]).then_inc(
                            sm("l6a"), 16)
                for n in range(64):
                    ht, tt = divmod(n, 8)
                    g = th * 64 + n
                    r0 = th * 1024 + tt * 128
                    if g >= 2:
                        sy.wait_ge(sm("v6"), g - 1)
                    sy.dma_start(x4_b[g % 2][:],
                                 x_own[r0:r0 + 128, ht * 512:(ht + 1) * 512]).then_inc(
                                     psem("l6x", g), 16)
                    sy.wait_ge(sm("v6"), g + 1)
                    sy.dma_start(ATT[r0:r0 + 128, ht * 512:(ht + 1) * 512],
                                 at4_b[g % 2][:]).then_inc(psem("st6", g), 16)
            sy.wait_ge(psem("st6", 0), 16 * 64)
            sy.wait_ge(psem("st6", 1), 16 * 64)

        def p4_g(g):
            for th in range(2):
                for ht in range(8):
                    n = th * 8 + ht
                    if n >= 2:
                        g.wait_ge(sm("t6"), 8 * (n - 1))
                    g.dma_start(woc[n % 2][:],
                                wo.rearrange("(a p) m -> p a m", p=128)
                                [:, :, ht * 512:(ht + 1) * 512]).then_inc(psem("wl6", n), 16)

        def p4_t(t):
            for th in range(2):
                t.wait_ge(sm("l6a"), 16 * 4 * (th + 1))
                for n in range(64):
                    ht, tt = divmod(n, 8)
                    g = th * 64 + n
                    wn = th * 8 + ht
                    if tt == 0:
                        t.wait_ge(psem("wl6", wn), 16 * (wn // 2 + 1))
                    slot = PS6[g % 6]
                    if g >= 6:
                        t.wait_ge(sm("v6"), g - 5)
                    for fc in range(NDC):
                        mi = t.matmul(slot,
                                      au_res[:, fc * 1024 + tt * 128: fc * 1024 + tt * 128 + 128],
                                      woc[wn % 2][:, fc * 512:(fc + 1) * 512],
                                      start=(fc == 0), stop=(fc == NDC - 1))
                    mi.then_inc(sm("t6"), 1)

        def p4_v(v):
            for g in range(128):
                slot = PS6[g % 6]
                v.wait_ge(sm("t6"), g + 1)
                v.wait_ge(psem("l6x", g), 16 * (g // 2 + 1))
                if g >= 2:
                    v.wait_ge(psem("st6", g), 16 * (g // 2))
                v.tensor_tensor(at4_b[g % 2][:], slot, x4_b[g % 2][:],
                                op=ALU.add).then_inc(sm("v6"), 1)
            v.wait_ge(sm("v6"), 128)

        def p5a_sy(sy):
            for j in range(4):
                sy.dma_start(
                    xn5.ap().rearrange("p (a m) -> p a m", a=NDC)[:, j * 8:(j + 1) * 8, :],
                    xn_view(slice(j * 8, (j + 1) * 8), 0, T)).then_inc(sm("l7"), 16)
            sy.dma_start(gsc_sb[:], gsc.rearrange("o (c p) -> p (o c)", p=128),
                         allow_slow_non_contiguous=True).then_inc(sm("l7"), 16)
            sy.dma_start(usc_sb[:], usc.rearrange("o (c p) -> p (o c)", p=128),
                         allow_slow_non_contiguous=True).then_inc(sm("l7"), 16)
            n = 0
            for ic in range(NI):
                for tt in range(4):
                    sy.wait_ge(sm("v7"), n + 2)
                    sy.dma_start(GU[2 * tt, ic],
                                 gu5_b[n % 4][:, 0:256]).then_inc(psem("st7", n), 16)
                    sy.dma_start(GU[2 * tt + 1, ic],
                                 gu5_b[n % 4][:, 256:512]).then_inc(psem("st7", n), 16)
                    n += 1
            sy.wait_ge(psem("st7", 0), 32 * NI * 2)
            sy.wait_ge(psem("st7", 1), 32 * NI * 2)

        def p5a_g(g):
            for ic in range(NI):
                p = ic % 2
                if ic >= 2:
                    g.wait_ge(sm("wf7"), ic - 1)
                g.dma_start(wg5[p][:],
                            gate_t.rearrange("(a p) m -> p a m", p=128)
                            [:, :, ic * 128:(ic + 1) * 128]).then_inc(psem("wl7", ic), 16)
                g.dma_start(wu5[p][:],
                            up_t.rearrange("(a p) m -> p a m", p=128)
                            [:, :, ic * 128:(ic + 1) * 128]).then_inc(psem("wl7", ic), 16)

        def p5a_v(v):
            v.wait_ge(sm("l7"), 16 * 6)
            v.tensor_tensor(sc5_sb[:], gsc_sb[:], usc_sb[:],
                            op=ALU.mult).then_inc(sm("v7"), 1)
            n = 0
            for ic in range(NI):
                for tt in range(4):
                    pp = n % 2
                    v.wait_ge(sm("t7"), n + 1)
                    v.wait_ge(sm("a7"), n + 1)
                    if n >= 4:
                        v.wait_ge(psem("st7", n), 32 * (n // 2 - 1))
                    v.scalar_tensor_tensor(gu5_b[n % 4][:],
                                           (PA if pp == 0 else PBk)[:, 0:512],
                                           sc5_sb[:, ic:ic + 1], u5_b[pp][:],
                                           op0=ALU.mult, op1=ALU.mult).then_inc(sm("v7"), 1)
                    n += 1
            v.wait_ge(sm("v7"), NI * 4 + 1)

        def p5a_s(s):
            n = 0
            for ic in range(NI):
                for tt in range(4):
                    pp = n % 2
                    s.wait_ge(sm("t7"), n + 1)
                    s.activation(u5_b[pp][:], (PA if pp == 0 else PBk)[:, 512:1024],
                                 AF.Copy).then_inc(sm("a7"), 1)
                    n += 1
            s.wait_ge(sm("a7"), NI * 4)

        def p5a_t(t):
            t.wait_ge(sm("l7"), 16 * 4)
            n = 0
            for ic in range(NI):
                p = ic % 2
                t.wait_ge(psem("wl7", ic), 32 * (ic // 2 + 1))
                for tt in range(4):
                    pp = n % 2
                    gslot = (PA if pp == 0 else PBk)[:, 0:512]
                    uslot = (PA if pp == 0 else PBk)[:, 512:1024]
                    if n >= 2:
                        t.wait_ge(sm("v7"), n)
                        t.wait_ge(sm("a7"), n - 1)
                    for dc in range(NDC):
                        t.matmul(gslot, wg5[p][:, dc * 128:(dc + 1) * 128],
                                 xn5[:, dc * T + tt * 512: dc * T + tt * 512 + 512],
                                 start=(dc == 0), stop=(dc == NDC - 1))
                    for dc in range(NDC):
                        mi = t.matmul(uslot, wu5[p][:, dc * 128:(dc + 1) * 128],
                                      xn5[:, dc * T + tt * 512: dc * T + tt * 512 + 512],
                                      start=(dc == 0), stop=(dc == NDC - 1))
                    mi.then_inc(sm("t7"), 1)
                    n += 1
                t.nop().then_inc(sm("wf7"), 1)

        NHT = 16
        NTB = 8
        GPH = 2 * NTB

        def p5b_sy(sy):
            n = 0
            for ht in range(NHT):
                for tb in range(NTB):
                    gb = ht * NTB + tb
                    if gb >= 2:
                        sy.wait_ge(sm("t8"), 2 * (gb - 1))
                    for j in (2, 3):
                        lo = j * 22
                        hi = min(NI, lo + 22)
                        sy.dma_start(
                            gub[tb % 2].ap().rearrange("p (a m) -> p a m", a=NI)
                            [:, lo:hi, :],
                            GU[tb].rearrange("a p m -> p a m")[:, lo:hi, :]).then_inc(
                                sems["gl8b" if gb % 2 == 0 else "gl8d"], 16)
                    for ts in range(2):
                        r0 = tb * 256 + ts * 128
                        if n >= 2:
                            sy.wait_ge(sm("v8"), n - 1)
                        sy.dma_start(at5_b[n % 2][:],
                                     ATT[r0:r0 + 128, ht * 256:(ht + 1) * 256]).then_inc(
                                         psem("l8", n), 16)
                        sy.wait_ge(sm("v8"), n + 1)
                        sy.dma_start(out[r0:r0 + 128, ht * 256:(ht + 1) * 256],
                                     o5_b[n % 2][:]).then_inc(psem("st8", n), 16)
                        n += 1
            sy.wait_ge(psem("st8", 0), 16 * NHT * NTB)
            sy.wait_ge(psem("st8", 1), 16 * NHT * NTB)

        def p5b_g(g):
            for ht in range(NHT):
                if ht >= 1:
                    g.wait_ge(sm("v8d"), ht)
                g.dma_start(dscb_sl[:],
                            bass.AP(dsc.tensor, ht * 256, [[0, 128], [1, 256]]),
                            ).then_inc(sm("l8d"), 16)
                if ht >= 2:
                    g.wait_ge(sm("t8"), GPH * (ht - 1))
                for j in range(8):
                    lo = j * 11
                    hi = min(NI, lo + 11)
                    g.dma_start(
                        dcol[ht % 2].ap().rearrange("p (a m) -> p a m", a=NI)[:, lo:hi, :],
                        down_t.rearrange("(a p) m -> p a m", p=128)
                        [:, lo:hi, ht * 256:(ht + 1) * 256]).then_inc(psem("wl8", ht), 16)

        def p5b_s(s):
            for ht in range(NHT):
                for tb in range(NTB):
                    gb = ht * NTB + tb
                    if gb >= 2:
                        s.wait_ge(sm("t8"), 2 * (gb - 1))
                    for j in (0, 1):
                        lo = j * 22
                        hi = min(NI, lo + 22)
                        s.dma_start(
                            gub[tb % 2].ap().rearrange("p (a m) -> p a m", a=NI)
                            [:, lo:hi, :],
                            GU[tb].rearrange("a p m -> p a m")[:, lo:hi, :]).then_inc(
                                sems["gl8" if gb % 2 == 0 else "gl8c"], 16)

        def p5b_t(t):
            qsplit = [0, 22, 44, 66, NI]
            for ht in range(NHT):
                t.wait_ge(sm("v8d"), ht + 1)
                for tb in range(NTB):
                    gb = ht * NTB + tb
                    for ts in range(2):
                        n = 2 * gb + ts
                        slot = PS6[n % 6][:, 0:256]
                        if n >= 6:
                            t.wait_ge(sm("v8"), n - 5)
                        t.wait_ge(sems["gl8" if gb % 2 == 0 else "gl8c"],
                                  32 * (gb // 2 + 1))
                        t.wait_ge(sems["gl8b" if gb % 2 == 0 else "gl8d"],
                                  32 * (gb // 2 + 1))
                        for jq in range(4):
                            for ic in range(qsplit[jq], qsplit[jq + 1]):
                                mi = t.matmul(slot,
                                              gub[tb % 2][:, ic * 256 + ts * 128:
                                                           ic * 256 + ts * 128 + 128],
                                              dcol[ht % 2][:, ic * 256:(ic + 1) * 256],
                                              start=(ic == 0), stop=(ic == NI - 1))
                        mi.then_inc(sm("t8"), 1)

        def p5b_v(v):
            n = 0
            for ht in range(NHT):
                v.wait_ge(sm("l8d"), 16 * (ht + 1))
                v.wait_ge(psem("wl8", ht), 16 * 8 * (ht // 2 + 1))
                _dap = dscb_sl.ap()
                v.tensor_tensor(
                    dcol[ht % 2].ap().rearrange("p (a m) -> p a m", a=NI),
                    dcol[ht % 2].ap().rearrange("p (a m) -> p a m", a=NI),
                    bass.AP(_dap.tensor, _dap.offset,
                            [_dap.ap[0], [0, NI], _dap.ap[1]]),
                    op=ALU.mult).then_inc(sm("v8d"), 1)
                for tb in range(NTB):
                    for ts in range(2):
                        slot = PS6[n % 6][:, 0:256]
                        v.wait_ge(sm("t8"), n + 1)
                        v.wait_ge(psem("l8", n), 16 * (n // 2 + 1))
                        if n >= 2:
                            v.wait_ge(psem("st8", n), 16 * (n // 2))
                        v.tensor_tensor(o5_b[n % 2][:], slot, at5_b[n % 2][:],
                                        op=ALU.add).then_inc(sm("v8"), 1)
                        n += 1
            v.wait_ge(sm("v8"), NHT * NTB * 2)

        P = phase_limit

        @block.gpsimd
        def _(g):
            ph = 0
            if P > 0: p0_g(g); ph += 1; barrier(g, ph)
            if P > 1: ph += 1; barrier(g, ph)
            if P > 2: p2q_g(g); ph += 1; barrier(g, ph)
            if P > 3: p2l_g(g); ph += 1; barrier(g, ph)
            if P > 4: p3_g(g); ph += 1; barrier(g, ph)
            if P > 5: p4_g(g); ph += 1; barrier(g, ph)
            if P > 6: p5a_g(g); ph += 1; barrier(g, ph)
            if P > 7: p5b_g(g); ph += 1; barrier(g, ph)

        @block.sync
        def _(sy):
            ph = 0
            if P > 0: ph += 1; barrier(sy, ph)
            if P > 1: p1_sy(sy); ph += 1; barrier(sy, ph)
            if P > 2: p2q_sy(sy); ph += 1; barrier(sy, ph)
            if P > 3: p2l_sy(sy); ph += 1; barrier(sy, ph)
            if P > 4: p3_sy(sy); ph += 1; barrier(sy, ph)
            if P > 5: p4_sy(sy); ph += 1; barrier(sy, ph)
            if P > 6: p5a_sy(sy); ph += 1; barrier(sy, ph)
            if P > 7: p5b_sy(sy); ph += 1; barrier(sy, ph)

        @block.vector
        def _(v):
            ph = 0
            if P > 0: p0_v(v); ph += 1; barrier(v, ph)
            if P > 1: p1_v(v); ph += 1; barrier(v, ph)
            if P > 2: p2q_v(v); ph += 1; barrier(v, ph)
            if P > 3: p2l_v(v); ph += 1; barrier(v, ph)
            if P > 4: p3_v(v); ph += 1; barrier(v, ph)
            if P > 5: p4_v(v); ph += 1; barrier(v, ph)
            if P > 6: p5a_v(v); ph += 1; barrier(v, ph)
            if P > 7: p5b_v(v); ph += 1; barrier(v, ph)

        @block.scalar
        def _(s):
            ph = 0
            if P > 0: p0_s(s); ph += 1; barrier(s, ph)
            if P > 1: p1_s(s); ph += 1; barrier(s, ph)
            if P > 2: p2q_s(s); ph += 1; barrier(s, ph)
            if P > 3: ph += 1; barrier(s, ph)
            if P > 4: p3_s(s); ph += 1; barrier(s, ph)
            if P > 5: ph += 1; barrier(s, ph)
            if P > 6: p5a_s(s); ph += 1; barrier(s, ph)
            if P > 7: p5b_s(s); ph += 1; barrier(s, ph)

        @block.tensor
        def _(t):
            ph = 0
            if P > 0: p0_t(t); ph += 1; barrier(t, ph)
            if P > 1: p1_t(t); ph += 1; barrier(t, ph)
            if P > 2: p2q_t(t); ph += 1; barrier(t, ph)
            if P > 3: p2l_t(t); ph += 1; barrier(t, ph)
            if P > 4: p3_t(t); ph += 1; barrier(t, ph)
            if P > 5: p4_t(t); ph += 1; barrier(t, ph)
            if P > 6: p5a_t(t); ph += 1; barrier(t, ph)
            if P > 7: p5b_t(t); ph += 1; barrier(t, ph)

    return nc


def make_in_maps(x, Wq, Wc, Wk, Wv, Wo, gwq, gs, uwq, us, dwq, ds):
    gate_tt = np.ascontiguousarray(np.asarray(gwq).T)
    up_tt = np.ascontiguousarray(np.asarray(uwq).T)
    down_tt = np.ascontiguousarray(np.asarray(dwq).T)
    shared = dict(
        wq=np.ascontiguousarray(Wq, np.float32), wc=np.ascontiguousarray(Wc, np.float32),
        wk=np.ascontiguousarray(Wk, np.float32), wv=np.ascontiguousarray(Wv, np.float32),
        wo=np.ascontiguousarray(Wo, np.float32),
        gate_t=gate_tt, up_t=up_tt, down_t=down_tt,
        gsc=np.ascontiguousarray(gs, np.float32).reshape(1, -1),
        usc=np.ascontiguousarray(us, np.float32).reshape(1, -1),
        dsc=np.ascontiguousarray(ds, np.float32).reshape(1, -1),
    )
    x = np.asarray(x, np.float32)
    in_maps = []
    for c in range(8):
        b, h = c // 2, c % 2
        m = dict(shared)
        m["x_own"] = np.ascontiguousarray(x[b, h * T:(h + 1) * T, :])
        m["x_oth"] = np.ascontiguousarray(x[b, (1 - h) * T:(2 - h) * T, :])
        in_maps.append(m)
    return in_maps


def kernel(L0_attn_hidden_states, L0_attn_Wq, L0_attn_Wc, L0_attn_Wk, L0_attn_Wv,
           L0_attn_Wo, L0_ffn_gate_wq, L0_ffn_gate_scale, L0_ffn_up_wq,
           L0_ffn_up_scale, L0_ffn_down_wq, L0_ffn_down_scale):
    in_maps = make_in_maps(L0_attn_hidden_states, L0_attn_Wq, L0_attn_Wc, L0_attn_Wk,
                           L0_attn_Wv, L0_attn_Wo, L0_ffn_gate_wq, L0_ffn_gate_scale,
                           L0_ffn_up_wq, L0_ffn_up_scale, L0_ffn_down_wq,
                           L0_ffn_down_scale)
    nc = build()
    res = run_bass_kernel_spmd(nc, in_maps, core_ids=list(range(8)))
    outp = np.empty((4, S, D), np.float32)
    for c in range(8):
        b, h = c // 2, c % 2
        outp[b, h * T:(h + 1) * T, :] = res.results[c]["out"]
    return outp


# version 07
# speedup vs baseline: 1.1631x; 1.1631x over previous
import contextlib
import math
import numpy as np
import concourse.bass as bass
import concourse.mybir as mybir
from concourse.bass_utils import run_bass_kernel_spmd

F32 = mybir.dt.float32
BF16 = mybir.dt.bfloat16
I32 = mybir.dt.int32
AF = mybir.ActivationFunctionType
ALU = mybir.AluOpType
AX = mybir.AxisListType

D = 4096
S = 4096
T = 2048
NH = 32
DL = 512
NI = 86
NDC = 32
EPS = 1e-5
ISQ = 1.0 / math.sqrt(128.0)
TWO_PI = 2.0 * math.pi
LN1E4 = math.log(10000.0)

SBUF_BYTES = 229376
WIN_BASE = 40960


class Win:

    def __init__(self, nc):
        self.nc = nc
        self.off = WIN_BASE

    def alloc(self, name, shape, dtype):
        esz = 2 if dtype == BF16 else 4
        nbytes = int(np.prod(shape[1:])) * esz
        off = (self.off + 31) // 32 * 32
        h = self.nc.alloc_sbuf_tensor_at(name, shape, dtype, offset=off)
        self.off = off + nbytes
        assert self.off <= SBUF_BYTES, f"SBUF overflow at {name}: {self.off}"
        return h

    def reset(self, off=WIN_BASE):
        self.off = off


def build(phase_limit=99, debug=False, norope=False):
    nc = bass.Bass("TRN2", target_bir_lowering=False)

    x_own = nc.dram_tensor("x_own", [T, D], F32, kind="ExternalInput").ap()
    x_oth = nc.dram_tensor("x_oth", [T, D], F32, kind="ExternalInput").ap()
    wq = nc.dram_tensor("wq", [D, D], F32, kind="ExternalInput").ap()
    wc = nc.dram_tensor("wc", [D, DL], F32, kind="ExternalInput").ap()
    wk = nc.dram_tensor("wk", [DL, D], F32, kind="ExternalInput").ap()
    wv = nc.dram_tensor("wv", [DL, D], F32, kind="ExternalInput").ap()
    wo = nc.dram_tensor("wo", [D, D], F32, kind="ExternalInput").ap()
    gate_t = nc.dram_tensor("gate_t", [D, NI * 128], I32, kind="ExternalInput").ap()
    up_t = nc.dram_tensor("up_t", [D, NI * 128], I32, kind="ExternalInput").ap()
    down_t = nc.dram_tensor("down_t", [NI * 128, D], I32, kind="ExternalInput").ap()
    gsc = nc.dram_tensor("gsc", [1, NI * 128], F32, kind="ExternalInput").ap()
    usc = nc.dram_tensor("usc", [1, NI * 128], F32, kind="ExternalInput").ap()
    dsc = nc.dram_tensor("dsc", [1, D], F32, kind="ExternalInput").ap()

    out = nc.dram_tensor("out", [T, D], F32, kind="ExternalOutput").ap()

    def scratch(name, shape, dtype):
        kind = "ExternalOutput" if debug else "Internal"
        return nc.dram_tensor(name, shape, dtype, kind=kind).ap()

    XN = scratch("XN", [D, S], BF16)
    QR = scratch("QR", [D, T], BF16)
    AU = scratch("AU", [D, T], BF16)
    ATT = scratch("ATT", [T, D], F32)
    GU = scratch("GU", [T // 256, NI, 128, 256], BF16)
    CS = scratch("CS", [2, 128, S], BF16) if debug else None
    MVD = scratch("MVD", [128, 3 * 32], F32) if debug else None
    PB = nc.dram_tensor("PB", [1, 1], I32)

    es = contextlib.ExitStack()
    with es:
        E = es.enter_context

        ident_f = E(nc.sbuf_tensor("ident_f", [128, 128], F32))
        ones_cb = E(nc.sbuf_tensor("ones_cb", [128, 1], BF16))
        ones_rf = E(nc.sbuf_tensor("ones_rf", [1, 128], F32))
        b0_i = E(nc.sbuf_tensor("b0_i", [1, 1], I32))
        b0_f = E(nc.sbuf_tensor("b0_f", [1, 1], F32))
        cos_t = E(nc.sbuf_tensor("cos_t", [128, S], BF16))
        sinf_t = E(nc.sbuf_tensor("sinf_t", [128, S], BF16))
        qmax_h = E(nc.sbuf_tensor("qmax_h", [1, NH], F32))
        halfpi_t = E(nc.sbuf_tensor("halfpi_t", [128, 1], F32))
        eps_t = E(nc.sbuf_tensor("eps_t", [128, 1], F32))

        used = SBUF_BYTES - nc.sbuf_bytes_remaining
        assert used <= WIN_BASE, f"persistent SBUF {used} exceeds window base {WIN_BASE}"

        PA = E(nc.psum_tensor("PA", [128, 1024], F32))
        PBk = E(nc.psum_tensor("PBk", [128, 1024], F32))
        PC = E(nc.psum_tensor("PC", [128, 1024], F32))
        SM = E(nc.psum_tensor("SM", [1, 1024], F32))
        PS6 = [PA[:, 0:512], PA[:, 512:1024], PBk[:, 0:512], PBk[:, 512:1024],
               PC[:, 0:512], PC[:, 512:1024]]

        sem_names = [
            "bar",
            "g0", "t0", "v0", "a0",
            "l1", "l1a", "l1b", "v1", "a1", "t1", "ve1", "st1",
            "l2", "wl2", "wf2", "t2", "a2", "v2", "a2b", "t2b", "v2b", "st2",
            "l3", "wl3", "t3", "v3",
            "wl4", "ql4", "t4", "a4", "t4b", "a4b", "v4", "a4c", "t4c",
            "t5", "a5", "t5b", "v5b", "t5c", "v5", "st5", "a5r",
            "l6", "wl6", "t6", "v6", "st6",
            "l7", "wl7", "wf7", "t7", "a7", "v7", "st7",
            "l8", "l8d", "wl8", "gl8", "gl8b", "t8", "v8", "st8", "t8d", "v8d",
            "st1a", "st1b", "st2a", "st2b", "wl2a", "wl2b", "wl4a", "wl4b",
            "ql4a", "ql4b", "st5a", "st5b", "l6a", "l6xa", "l6xb", "wl6a",
            "wl6b", "st6a", "st6b", "wl7a", "wl7b", "st7a", "st7b", "wl8a",
            "wl8b", "gl8c", "gl8d", "l8a", "l8b", "st8a", "st8b",
        ]
        sems = {n: E(nc.semaphore(n)) for n in sem_names}
        sm = lambda n: sems[n]
        psem = lambda base, i: sems[base + ("a" if i % 2 == 0 else "b")]

        win = Win(nc)

        win.reset()
        iota_r = win.alloc("iota_r", [1, S], F32)
        pos_r = win.alloc("pos_r", [1, S], F32)
        dmod_r = win.alloc("dmod_r", [1, 128], F32)
        invf_r = win.alloc("invf_r", [1, 128], F32)
        p0_tq = win.alloc("p0_tq", [128, 512], F32)
        p0_tqc = win.alloc("p0_tqc", [128, 512], F32)
        p0_ti = win.alloc("p0_ti", [128, 512], I32)
        p0_tf = win.alloc("p0_tf", [128, 512], F32)
        p0_ci = win.alloc("p0_ci", [128, 512], I32)
        p0_cf = win.alloc("p0_cf", [128, 512], F32)
        p0_red = win.alloc("p0_red", [128, 512], F32)
        p0_redc = win.alloc("p0_redc", [128, 512], F32)
        p0_sin = win.alloc("p0_sin", [128, 512], F32)
        p0_ones = win.alloc("p0_ones", [128, 128], F32)

        win.reset()
        x_b = [win.alloc(f"x_b{i}", [128, D], F32) for i in range(3)]
        xnf_b = [win.alloc(f"xnf_b{i}", [128, D], F32) for i in range(2)]
        bns_b = win.alloc("bns_b", [128, 48], F32)
        mv_b = win.alloc("mv_b", [128, 2], F32)
        sq_b = win.alloc("sq_b", [128, 1], F32)
        rstd_b = win.alloc("rstd_b", [128, 1], F32)
        ev_b = [win.alloc(f"ev_b{i}", [128, 512], BF16) for i in range(2)]
        mvd_sb = win.alloc("mvd_sb", [128, 96], F32) if debug else None

        win.reset()
        xnq = win.alloc("xnq", [128, NDC * T], BF16)
        wqc = [win.alloc(f"wqc{i}", [128, NDC * 128], BF16) for i in range(2)]
        qev_b = [win.alloc(f"qev_b{i}", [128, 512], BF16) for i in range(2)]
        shq_b = win.alloc("shq_b", [128, 512], BF16)
        t1_b = win.alloc("t1_b", [128, 512], BF16)
        t2_b = win.alloc("t2_b", [128, 512], BF16)
        qr_b = [win.alloc(f"qr_b{i}", [128, 512], BF16) for i in range(2)]
        qsq_b = win.alloc("qsq_b", [128, 512], BF16)
        qmx_b = win.alloc("qmx_b", [1, 128], F32)

        win.reset()
        lat = win.alloc("lat", [128, 4 * S], BF16)
        p3_base = win.off
        wc_sb = win.alloc("wc_sb", [128, NDC * 512], BF16)
        xnl_b = [win.alloc(f"xnl_b{i}", [128, NDC * 512], BF16) for i in range(2)]

        win.reset(p3_base)
        wkc = [win.alloc(f"wkc{i}", [128, 4 * 128], BF16) for i in range(2)]
        wvc = [win.alloc(f"wvc{i}", [128, 4 * 128], BF16) for i in range(2)]
        qrh = [win.alloc(f"qrh{i}", [128, T], BF16) for i in range(2)]
        k_sb = win.alloc("k_sb", [128, S], BF16)
        shk_sb = win.alloc("shk_sb", [128, S], BF16)
        kt1_sb = win.alloc("kt1_sb", [128, S], BF16)
        kr_sb = win.alloc("kr_sb", [128, S], BF16)
        v_sb = win.alloc("v_sb", [128, S], BF16)
        ksq_sb = win.alloc("ksq_sb", [128, 512], BF16)
        kms_sb = win.alloc("kms_sb", [1, 8], F32)
        kmax_sb = win.alloc("kmax_sb", [1, 1], F32)
        msq_sb = win.alloc("msq_sb", [1, 1], F32)
        m_sb = win.alloc("m_sb", [1, 1], F32)
        lnm_sb = win.alloc("lnm_sb", [1, 1], F32)
        bias_sb = win.alloc("bias_sb", [128, 1], F32)
        e_sb = [win.alloc(f"e_sb{i}", [128, 1024], BF16) for i in range(2)]
        recip_sb = win.alloc("recip_sb", [1, 1024], F32)
        rb_sb = win.alloc("rb_sb", [128, 1024], BF16)
        au_b = [win.alloc(f"au_b{i}", [128, T], BF16) for i in range(2)]

        win.reset()
        au_res = win.alloc("au_res", [128, NDC * 1024], BF16)
        woc = [win.alloc(f"woc{i}", [128, NDC * 512], BF16) for i in range(2)]
        x4_b = [win.alloc(f"x4_b{i}", [128, 512], F32) for i in range(2)]
        at4_b = [win.alloc(f"at4_b{i}", [128, 512], F32) for i in range(2)]

        win.reset()
        xn5 = win.alloc("xn5", [128, NDC * T], BF16)
        wg5 = [win.alloc(f"wg5{i}", [128, NDC * 128], BF16) for i in range(2)]
        wu5 = [win.alloc(f"wu5{i}", [128, NDC * 128], BF16) for i in range(2)]
        u5_b = [win.alloc(f"u5_b{i}", [128, 512], F32) for i in range(2)]
        gu5_b = [win.alloc(f"gu5_b{i}", [128, 512], BF16) for i in range(4)]
        gsc_sb = win.alloc("gsc_sb", [128, NI], F32)
        usc_sb = win.alloc("usc_sb", [128, NI], F32)
        sc5_sb = win.alloc("sc5_sb", [128, NI], F32)

        win.reset()
        dcol = [win.alloc(f"dcol{i}", [128, NI * 256], BF16) for i in range(2)]
        gub = [win.alloc(f"gub{i}", [128, NI * 256], BF16) for i in range(2)]
        dscb_sl = win.alloc("dscb_sl", [128, 256], F32)
        at5_b = [win.alloc(f"at5_b{i}", [128, 256], F32) for i in range(2)]
        o5_b = [win.alloc(f"o5_b{i}", [128, 256], F32) for i in range(2)]

        block = E(nc.Block())

        def barrier(eng, idx):
            eng.nop().then_inc(sm("bar"), 1)
            eng.wait_ge(sm("bar"), 5 * idx)

        def xn_view(dc, t0, tn):
            return XN.rearrange("(a p) m -> p a m", p=128)[:, dc, t0:t0 + tn]

        def p0_g(g):
            pid = g.partition_id()
            with g.register("r0") as r0:
                g.reg_mov(r0, (pid % 2) * 2048)
                g.reg_save(PB.ap()[0:1, 0:1], r0)
            g.nop().then_inc(sm("g0"), 1)
            g.wait_ge(sm("g0"), 1)
            g.dma_start(b0_i[:], PB.ap()).then_inc(sm("g0"), 16)
            g.memset(halfpi_t[:], math.pi / 2).then_inc(sm("g0"), 1)
            g.memset(eps_t[:], EPS).then_inc(sm("g0"), 1)
            g.iota(iota_r[:], pattern=[[1, S]], base=0, channel_multiplier=0,
                   allow_small_or_imprecise_dtypes=True).then_inc(sm("g0"), 1)
            g.iota(dmod_r[:], pattern=[[0, 2], [1, 64]], base=0, channel_multiplier=0,
                   allow_small_or_imprecise_dtypes=True).then_inc(sm("g0"), 1)
            g.memset(p0_ones[:], 1.0).then_inc(sm("g0"), 1)
            g.wait_ge(sm("g0"), 22)
            g.affine_select(ident_f[:], p0_ones[:], pattern=[[1, 128]],
                            compare_op=ALU.is_equal, fill=0.0,
                            base=0, channel_multiplier=-1).then_inc(sm("g0"), 1)
            g.memset(ones_cb[:], 1.0).then_inc(sm("g0"), 1)
            g.memset(ones_rf[:], 1.0).then_inc(sm("g0"), 1)
            g.wait_ge(sm("g0"), 25)

        def p0_v(v):
            v.wait_ge(sm("g0"), 20)
            v.tensor_copy(b0_f[:], b0_i[:]).then_inc(sm("v0"), 1)
            v.wait_ge(sm("v0"), 1)
            v.tensor_scalar(pos_r[0:1, 0:T], iota_r[0:1, 0:T], b0_f[0:1, 0:1], None,
                            op0=ALU.add).then_inc(sm("v0"), 1)
            v.tensor_scalar(pos_r[0:1, T:S], iota_r[0:1, T:S], b0_f[0:1, 0:1], None,
                            op0=ALU.subtract).then_inc(sm("v0"), 1)
            for pt in range(8):
                v.wait_ge(sm("t0"), pt + 1)
                v.tensor_scalar(p0_tq[:], PA[:, 0:512], 1.0 / TWO_PI, None,
                                op0=ALU.mult).then_inc(sm("v0"), 1)
                v.tensor_scalar(p0_tqc[:], PA[:, 0:512], 1.0 / TWO_PI, 0.25,
                                op0=ALU.mult, op1=ALU.add).then_inc(sm("v0"), 1)
                v.tensor_copy(p0_ti[:], p0_tq[:]).then_inc(sm("v0"), 1)
                v.tensor_copy(p0_tf[:], p0_ti[:]).then_inc(sm("v0"), 1)
                v.tensor_copy(p0_ci[:], p0_tqc[:]).then_inc(sm("v0"), 1)
                v.tensor_copy(p0_cf[:], p0_ci[:]).then_inc(sm("v0"), 1)
                v.scalar_tensor_tensor(p0_red[:], p0_tf[:], -TWO_PI, PA[:, 0:512],
                                       op0=ALU.mult, op1=ALU.add).then_inc(sm("v0"), 1)
                v.scalar_tensor_tensor(p0_redc[:], p0_cf[:], -TWO_PI, PA[:, 0:512],
                                       op0=ALU.mult, op1=ALU.add).then_inc(sm("v0"), 1)
                v.wait_ge(sm("a0"), 3 + pt * 2)
                sl = slice(pt * 512, (pt + 1) * 512)
                v.tensor_copy(sinf_t[64:128, sl], p0_sin[64:128, :]).then_inc(sm("v0"), 1)
                v.tensor_scalar(sinf_t[0:64, sl], p0_sin[0:64, :], -1.0, None,
                                op0=ALU.mult).then_inc(sm("v0"), 1)
            if True:
                v.wait_ge(sm("v0"), 83)

        def p0_s(s):
            s.wait_ge(sm("g0"), 21)
            s.activation(invf_r[:], dmod_r[:], AF.Exp,
                         scale=-LN1E4 / 64.0).then_inc(sm("a0"), 1)
            for pt in range(8):
                base = 3 + pt * 10
                sl = slice(pt * 512, (pt + 1) * 512)
                s.wait_ge(sm("v0"), base + 7)
                s.activation(p0_sin[:], p0_red[:], AF.Sin).then_inc(sm("a0"), 1)
                s.wait_ge(sm("v0"), base + 8)
                s.activation(cos_t[:, sl], p0_redc[:], AF.Sin,
                             bias=halfpi_t[:]).then_inc(sm("a0"), 1)
            if True:
                s.wait_ge(sm("a0"), 17)

        def p0_t(t):
            t.wait_ge(sm("a0"), 1)
            t.wait_ge(sm("v0"), 3)
            for pt in range(8):
                if pt > 0:
                    t.wait_ge(sm("v0"), 3 + (pt - 1) * 10 + 8)
                t.matmul(PA[:, 0:512], invf_r[:], pos_r[0:1, pt * 512:(pt + 1) * 512],
                         start=True, stop=True).then_inc(sm("t0"), 1)

        def p1_sy(sy):
            dbg_st = 0
            if debug:
                sy.dma_start(CS[0], cos_t[:]).then_inc(sm("st1"), 16)
                sy.dma_start(CS[1], sinf_t[:]).then_inc(sm("st1"), 16)
                dbg_st = 2
            for pt in range(32):
                for dcg in range(8):
                    n = pt * 8 + dcg
                    sy.wait_ge(sm("ve1"), n + 1)
                    sy.dma_start(
                        XN.rearrange("(a p) m -> p a m", p=128)
                        [:, dcg * 4:(dcg + 1) * 4, pt * 128:(pt + 1) * 128],
                        ev_b[dcg % 2].ap().rearrange("p (a m) -> p a m", a=4),
                    ).then_inc(psem("st1", n), 16)
            if debug:
                sy.wait_ge(sm("v1"), 128)
                sy.dma_start(MVD, mvd_sb[:]).then_inc(sm("st1"), 16)
                dbg_st += 1
            sy.wait_ge(psem("st1", 0), 16 * 128)
            sy.wait_ge(psem("st1", 1), 16 * 128)
            if dbg_st:
                sy.wait_ge(sm("st1"), 16 * dbg_st)

        def p1_g(g):
            for pt in range(32):
                src = x_own if pt < 16 else x_oth
                r0 = (pt % 16) * 128
                if pt >= 3:
                    g.wait_ge(sm("v1"), 4 * (pt - 3) + 4)
                g.dma_start(x_b[pt % 3][:], src[r0:r0 + 128, :]).then_inc(
                    psem("l1", pt), 16)

        def p1_v(v):
            for pt in range(32):
                v.wait_ge(psem("l1", pt), 16 * (pt // 2 + 1))
                for j in range(8):
                    mi = v.bn_stats(bns_b[:, j * 6:(j + 1) * 6],
                                    x_b[pt % 3][:, j * 512:(j + 1) * 512])
                mi.then_inc(sm("v1"), 1)
                v.wait_ge(sm("v1"), 4 * pt + 1)
                v.bn_aggr(mv_b[:], bns_b[:]).then_inc(sm("v1"), 1)
                v.wait_ge(sm("a1"), pt + 1)
                v.reciprocal(rstd_b[:], sq_b[:]).then_inc(sm("v1"), 1)
                v.wait_ge(sm("v1"), 4 * pt + 3)
                if pt >= 2:
                    v.wait_ge(sm("t1"), 8 * (pt - 2) + 8)
                v.tensor_scalar(xnf_b[pt % 2][:], x_b[pt % 3][:], mv_b[:, 0:1],
                                rstd_b[:], op0=ALU.subtract,
                                op1=ALU.mult).then_inc(sm("v1"), 1)
                if debug:
                    v.tensor_copy(mvd_sb[:, 3 * pt:3 * pt + 2], mv_b[:])
                    v.tensor_copy(mvd_sb[:, 3 * pt + 2:3 * pt + 3], rstd_b[:])
            v.wait_ge(sm("v1"), 128)

        def p1_s(s):
            for pt in range(32):
                s.wait_ge(sm("v1"), 4 * pt + 2)
                s.activation(sq_b[:], mv_b[:, 1:2], AF.Sqrt,
                             bias=eps_t[:]).then_inc(sm("a1"), 1)
                for dcg in range(8):
                    n = pt * 8 + dcg
                    s.wait_ge(sm("t1"), n + 1)
                    if n >= 2:
                        s.wait_ge(psem("st1", n), 16 * (n // 2))
                    s.activation(ev_b[dcg % 2][:],
                                 (PA if dcg % 2 == 0 else PBk)[:, 0:512],
                                 AF.Copy).then_inc(sm("ve1"), 1)
            s.wait_ge(sm("a1"), 32)
            s.wait_ge(sm("ve1"), 256)

        def p1_t(t):
            for pt in range(32):
                t.wait_ge(sm("v1"), 4 * pt + 4)
                for dcg in range(8):
                    n = pt * 8 + dcg
                    ps = (PA if dcg % 2 == 0 else PBk)
                    if n >= 2:
                        t.wait_ge(sm("ve1"), n - 1)
                    for j in range(4):
                        dc = dcg * 4 + j
                        mi = t.transpose(ps[:, j * 128:(j + 1) * 128],
                                         xnf_b[pt % 2][:, dc * 128:(dc + 1) * 128],
                                         ident_f[:])
                    mi.then_inc(sm("t1"), 1)

        def p2q_sy(sy):
            for j in range(4):
                sy.dma_start(
                    xnq.ap().rearrange("p (a m) -> p a m", a=NDC)[:, j * 8:(j + 1) * 8, :],
                    xn_view(slice(j * 8, (j + 1) * 8), 0, T)).then_inc(sm("l2"), 16)
            for n in range(128):
                fc, qt = divmod(n, 4)
                sy.wait_ge(sm("v2"), 5 * n + 5)
                sy.dma_start(QR[fc * 128:(fc + 1) * 128, qt * 512:(qt + 1) * 512],
                             qr_b[n % 2][:]).then_inc(psem("st2", n), 16)
            sy.wait_ge(psem("st2", 0), 16 * 64)
            sy.wait_ge(psem("st2", 1), 16 * 64)

        def p2q_g(g):
            for fc in range(NH):
                if fc >= 2:
                    g.wait_ge(sm("wf2"), fc - 1)
                g.dma_start(wqc[fc % 2][:],
                            wq.rearrange("(a p) m -> p a m", p=128)
                            [:, :, fc * 128:(fc + 1) * 128]).then_inc(psem("wl2", fc), 16)

        def p2q_t(t):
            t.wait_ge(sm("l2"), 64)
            for n in range(128):
                fc, qt = divmod(n, 4)
                slot = PS6[n % 6]
                if qt == 0:
                    t.wait_ge(psem("wl2", fc), 16 * (fc // 2 + 1))
                if n >= 6:
                    t.wait_ge(sm("a2"), n - 5)
                for dc in range(NDC):
                    mi = t.matmul(slot, wqc[fc % 2][:, dc * 128:(dc + 1) * 128],
                                  xnq[:, dc * T + qt * 512: dc * T + qt * 512 + 512],
                                  start=(dc == 0), stop=(dc == NDC - 1))
                mi.then_inc(sm("t2"), 1)
                if qt == 3:
                    t.nop().then_inc(sm("wf2"), 1)
                if n >= 1:
                    t.wait_ge(sm("a2b"), n)
                    if n >= 2:
                        t.wait_ge(sm("v2b"), n - 1)
                    t.matmul(SM[0:1, 0:512], ones_cb[:], qsq_b[:],
                             start=True, stop=True).then_inc(sm("t2b"), 1)
            t.wait_ge(sm("a2b"), 128)
            t.wait_ge(sm("v2b"), 127)
            t.matmul(SM[0:1, 0:512], ones_cb[:], qsq_b[:],
                     start=True, stop=True).then_inc(sm("t2b"), 1)

        def p2q_s(s):
            for n in range(128):
                slot = PS6[n % 6]
                s.wait_ge(sm("t2"), n + 1)
                s.activation(qev_b[n % 2][:], slot, AF.Copy).then_inc(sm("a2"), 1)
                s.wait_ge(sm("v2"), 5 * n + 5)
                if n >= 1:
                    s.wait_ge(sm("t2b"), n)
                s.activation(qsq_b[:], qr_b[n % 2][:], AF.Square).then_inc(sm("a2b"), 1)
            s.wait_ge(sm("a2"), 128)
            s.wait_ge(sm("a2b"), 128)

        def p2q_v(v):
            for n in range(128):
                fc, qt = divmod(n, 4)
                sl = slice(qt * 512, (qt + 1) * 512)
                v.wait_ge(sm("a2"), n + 1)
                v.tensor_copy(shq_b[0:64, :], qev_b[n % 2][64:128, :]).then_inc(sm("v2"), 1)
                v.tensor_copy(shq_b[64:128, :], qev_b[n % 2][0:64, :]).then_inc(sm("v2"), 1)
                v.tensor_tensor(t1_b[:], qev_b[n % 2][:], cos_t[:, sl],
                                op=ALU.mult).then_inc(sm("v2"), 1)
                v.tensor_tensor(t2_b[:], shq_b[:], sinf_t[:, sl],
                                op=ALU.mult).then_inc(sm("v2"), 1)
                if n >= 2:
                    v.wait_ge(psem("st2", n), 16 * (n // 2))
                    v.wait_ge(sm("a2b"), n - 1)
                if norope:
                    v.tensor_copy(qr_b[n % 2][:], qev_b[n % 2][:]).then_inc(sm("v2"), 1)
                else:
                    v.tensor_tensor(qr_b[n % 2][:], t1_b[:], t2_b[:],
                                    op=ALU.add).then_inc(sm("v2"), 1)
                if n >= 1:
                    v.wait_ge(sm("t2b"), n)
                    v.reduce_max(qmx_b[0:1, n - 1:n], SM[0:1, 0:512],
                                 axis=AX.X).then_inc(sm("v2b"), 1)
            v.wait_ge(sm("t2b"), 128)
            v.reduce_max(qmx_b[0:1, 127:128], SM[0:1, 0:512],
                         axis=AX.X).then_inc(sm("v2b"), 1)
            v.wait_ge(sm("v2b"), 128)
            for fc in range(NH):
                v.reduce_max(qmax_h[0:1, fc:fc + 1], qmx_b[0:1, fc * 4:(fc + 1) * 4],
                             axis=AX.X).then_inc(sm("v2b"), 1)
            v.wait_ge(sm("v2b"), 160)
            v.wait_ge(sm("v2"), 640)

        def p2l_sy(sy):
            for pt in range(8):
                if pt >= 1:
                    sy.wait_ge(sm("t3"), 4 * pt)
                for j in range(4):
                    sy.dma_start(
                        xnl_b[pt % 2].ap().rearrange("p (a m) -> p a m", a=NDC)
                        [:, j * 8:(j + 1) * 8, :],
                        xn_view(slice(j * 8, (j + 1) * 8), pt * 512, 512)).then_inc(
                            sm("l3"), 16)

        def p2l_g(g):
            g.dma_start(wc_sb[:],
                        wc.rearrange("(a p) m -> p a m", p=128)).then_inc(sm("wl3"), 16)

        def p2l_t(t):
            t.wait_ge(sm("wl3"), 16)
            for pt in range(8):
                t.wait_ge(sm("l3"), 16 * 4 * (pt + 1))
                for lc in range(4):
                    n = pt * 4 + lc
                    slot = PS6[n % 6]
                    if n >= 6:
                        t.wait_ge(sm("v3"), n - 5)
                    for dc in range(NDC):
                        mi = t.matmul(slot,
                                      wc_sb[:, dc * 512 + lc * 128: dc * 512 + lc * 128 + 128],
                                      xnl_b[pt % 2][:, dc * 512:(dc + 1) * 512],
                                      start=(dc == 0), stop=(dc == NDC - 1))
                    mi.then_inc(sm("t3"), 1)

        def p2l_v(v):
            for n in range(32):
                pt, lc = divmod(n, 4)
                slot = PS6[n % 6]
                v.wait_ge(sm("t3"), n + 1)
                v.tensor_copy(lat[:, lc * S + pt * 512: lc * S + pt * 512 + 512],
                              slot).then_inc(sm("v3"), 1)
            v.wait_ge(sm("v3"), 32)

        T4H, A4H, T4BH, A4BH = 8, 8, 8, 8
        V4H = 16
        A4CH = 10
        T4CH = 9
        T5H, A5H, T5BH = 64, 64, 64
        V5BH, T5CH, V5H = 2, 2, 2
        A5RH = 2

        def p3_g(g):
            for h in range(NH):
                if h >= 2:
                    g.wait_ge(sm("t4"), T4H * (h - 1))
                    g.wait_ge(sm("t4b"), T4BH * (h - 1))
                g.dma_start(wkc[h % 2][:],
                            wk.rearrange("(a p) m -> p a m", p=128)
                            [:, :, h * 128:(h + 1) * 128]).then_inc(psem("wl4", h), 16)
                g.dma_start(wvc[h % 2][:],
                            wv.rearrange("(a p) m -> p a m", p=128)
                            [:, :, h * 128:(h + 1) * 128]).then_inc(psem("wl4", h), 16)

        def p3_sy(sy):
            for h in range(NH):
                if h >= 2:
                    sy.wait_ge(sm("t5"), T5H * (h - 1))
                sy.dma_start(qrh[h % 2][:],
                             QR[h * 128:(h + 1) * 128, :]).then_inc(psem("ql4", h), 16)
                sy.wait_ge(sm("v5"), V5H * h + 2)
                sy.dma_start(AU[h * 128:(h + 1) * 128, :],
                             au_b[h % 2][:]).then_inc(psem("st5", h), 16)
            sy.wait_ge(psem("st5", 0), 16 * 16)
            sy.wait_ge(psem("st5", 1), 16 * 16)

        def p3_t(t):
            for h in range(NH):
                t.wait_ge(psem("wl4", h), 32 * (h // 2 + 1))
                t.wait_ge(psem("ql4", h), 16 * (h // 2 + 1))
                if h >= 1:
                    t.wait_ge(sm("v5"), V5H * h)
                for pt in range(8):
                    slot = PS6[pt % 4]
                    if pt >= 4:
                        t.wait_ge(sm("a4"), A4H * h + pt - 3)
                    for lc in range(4):
                        mi = t.matmul(slot, wkc[h % 2][:, lc * 128:(lc + 1) * 128],
                                      lat[:, lc * S + pt * 512: lc * S + pt * 512 + 512],
                                      start=(lc == 0), stop=(lc == 3))
                    mi.then_inc(sm("t4"), 1)
                for grp in range(8):
                    slot = PS6[4 + grp % 2]
                    if grp >= 2:
                        t.wait_ge(sm("a4b"), A4BH * h + grp - 1)
                    for j in range(4):
                        pc = grp * 4 + j
                        for lc in range(4):
                            mi = t.matmul(
                                slot[:, j * 128:(j + 1) * 128],
                                lat[:, lc * S + pc * 128: lc * S + pc * 128 + 128],
                                wvc[h % 2][:, lc * 128:(lc + 1) * 128],
                                start=(lc == 0), stop=(lc == 3))
                    mi.then_inc(sm("t4b"), 1)
                for j in range(8):
                    t.wait_ge(sm("a4c"), A4CH * h + j + 1)
                    if h >= 1 and j == 0:
                        t.wait_ge(sm("v5b"), V5BH * h)
                    if j >= 1:
                        t.wait_ge(sm("v4"), V4H * h + 5 + j)
                    t.matmul(SM[0:1, 0:512], ones_cb[:], ksq_sb[:],
                             start=True, stop=True).then_inc(sm("t4c"), 1)
                t.wait_ge(sm("a4c"), A4CH * h + 10)
                t.wait_ge(sm("a4"), A4H * h + 8)
                t.matmul(PA[:, 0:1], ones_rf[:], m_sb[0:1, 0:1],
                         start=True, stop=True).then_inc(sm("t4c"), 1)
                for qh in range(2):
                    q0 = qh * 1024
                    for kc in range(33):
                        if kc < 32:
                            sg = PBk if kc % 2 == 0 else PC
                            if kc == 0:
                                t.wait_ge(sm("v4"), V4H * (h + 1))
                                t.wait_ge(sm("a4b"), A4BH * h + 8)
                                if qh == 0:
                                    t.wait_ge(sm("v5"), V5H * h)
                                else:
                                    t.wait_ge(sm("v5"), V5H * h + 1)
                                    t.wait_ge(sm("a5r"), A5RH * h + 1)
                                t.wait_ge(sm("v5b"), V5BH * h + qh)
                            if kc >= 2:
                                t.wait_ge(sm("a5"), A5H * h + 32 * qh + kc - 1)
                            for half in range(2):
                                mi = t.matmul(sg[:, half * 512:(half + 1) * 512],
                                              kr_sb[:, kc * 128:(kc + 1) * 128],
                                              qrh[h % 2][:, q0 + half * 512: q0 + half * 512 + 512],
                                              start=True, stop=True)
                            mi.then_inc(sm("t5"), 1)
                        if kc >= 1:
                            kd = kc - 1
                            t.wait_ge(sm("a5"), A5H * h + 32 * qh + kd + 1)
                            for half in range(2):
                                t.matmul(PA[:, half * 512:(half + 1) * 512],
                                         v_sb[:, kd * 128:(kd + 1) * 128],
                                         e_sb[kd % 2][:, half * 512:(half + 1) * 512],
                                         start=(kd == 0), stop=(kd == 31))
                            for half in range(2):
                                mi = t.matmul(SM[0:1, half * 512:(half + 1) * 512],
                                              ones_cb[:],
                                              e_sb[kd % 2][:, half * 512:(half + 1) * 512],
                                              start=(kd == 0), stop=(kd == 31))
                            mi.then_inc(sm("t5b"), 1)
                    t.wait_ge(sm("v5b"), V5BH * h + qh + 1)
                    for half in range(2):
                        mi = t.matmul(PBk[:, half * 512:(half + 1) * 512], ones_rf[:],
                                      recip_sb[0:1, half * 512:(half + 1) * 512],
                                      start=True, stop=True)
                    mi.then_inc(sm("t5c"), 1)

        def p3_s(s):
            for h in range(NH):
                for pt in range(8):
                    slot = PS6[pt % 4]
                    s.wait_ge(sm("t4"), T4H * h + pt + 1)
                    s.activation(k_sb[:, pt * 512:(pt + 1) * 512], slot,
                                 AF.Copy).then_inc(sm("a4"), 1)
                for grp in range(8):
                    slot = PS6[4 + grp % 2]
                    s.wait_ge(sm("t4b"), T4BH * h + grp + 1)
                    s.activation(v_sb[:, grp * 512:(grp + 1) * 512], slot,
                                 AF.Copy).then_inc(sm("a4b"), 1)
                for j in range(8):
                    s.wait_ge(sm("a4"), A4H * h + j + 1)
                    if j >= 1:
                        s.wait_ge(sm("t4c"), T4CH * h + j)
                    s.activation(ksq_sb[:], k_sb[:, j * 512:(j + 1) * 512],
                                 AF.Square).then_inc(sm("a4c"), 1)
                s.wait_ge(sm("v4"), V4H * h + 15)
                s.activation(lnm_sb[:], msq_sb[:], AF.Ln).then_inc(sm("a4c"), 1)
                s.activation(m_sb[:], lnm_sb[:], AF.Exp, scale=0.5).then_inc(sm("a4c"), 1)
                for qh in range(2):
                    s.wait_ge(sm("v4"), V4H * (h + 1))
                    for kc in range(32):
                        sg = PBk if kc % 2 == 0 else PC
                        s.wait_ge(sm("t5"), T5H * h + 32 * qh + kc + 1)
                        if kc >= 2:
                            s.wait_ge(sm("t5b"), T5BH * h + 32 * qh + kc - 1)
                        s.activation(e_sb[kc % 2][:], sg[:, 0:1024], AF.Exp,
                                     bias=bias_sb[:],
                                     scale=ISQ).then_inc(sm("a5"), 1)
                    s.wait_ge(sm("t5b"), T5BH * h + 32 * (qh + 1))
                    s.activation(recip_sb[:], SM[0:1, :], AF.Ln).then_inc(sm("v5b"), 1)
                    s.wait_ge(sm("t5c"), T5CH * h + qh + 1)
                    s.activation(rb_sb[:], PBk[:, 0:1024], AF.Exp,
                                 scale=-1.0).then_inc(sm("a5r"), 1)
            s.wait_ge(sm("a5"), 2048)
            s.wait_ge(sm("a5r"), A5RH * NH)

        def p3_v(v):
            for h in range(NH):
                v.wait_ge(sm("a4"), A4H * h + 8)
                v.tensor_copy(shk_sb[0:64, :], k_sb[64:128, :]).then_inc(sm("v4"), 1)
                v.tensor_copy(shk_sb[64:128, :], k_sb[0:64, :]).then_inc(sm("v4"), 1)
                v.tensor_tensor(kt1_sb[:], k_sb[:], cos_t[:],
                                op=ALU.mult).then_inc(sm("v4"), 1)
                v.tensor_tensor(shk_sb[:], shk_sb[:], sinf_t[:],
                                op=ALU.mult).then_inc(sm("v4"), 1)
                v.tensor_tensor(kr_sb[:], kt1_sb[:], shk_sb[:],
                                op=ALU.add).then_inc(sm("v4"), 1)
                for j in range(8):
                    v.wait_ge(sm("t4c"), T4CH * h + j + 1)
                    v.reduce_max(kms_sb[0:1, j:j + 1], SM[0:1, 0:512],
                                 axis=AX.X).then_inc(sm("v4"), 1)
                v.wait_ge(sm("v4"), V4H * h + 13)
                v.reduce_max(kmax_sb[:], kms_sb[:], axis=AX.X).then_inc(sm("v4"), 1)
                v.wait_ge(sm("v4"), V4H * h + 14)
                v.tensor_tensor(msq_sb[:], kmax_sb[:], qmax_h[0:1, h:h + 1],
                                op=ALU.mult).then_inc(sm("v4"), 1)
                v.wait_ge(sm("t4c"), T4CH * h + 9)
                v.tensor_scalar(bias_sb[:], PA[:, 0:1], -ISQ, None,
                                op0=ALU.mult).then_inc(sm("v4"), 1)
                for qh in range(2):
                    v.wait_ge(sm("a5r"), A5RH * h + qh + 1)
                    v.wait_ge(sm("t5b"), T5BH * h + 32 * (qh + 1))
                    if h >= 2 and qh == 0:
                        v.wait_ge(psem("st5", h), 16 * (h // 2))
                    v.tensor_tensor(au_b[h % 2][:, qh * 1024:(qh + 1) * 1024],
                                    PA[:, 0:1024], rb_sb[:],
                                    op=ALU.mult).then_inc(sm("v5"), 1)
            v.wait_ge(sm("v5"), V5H * NH)
            v.wait_ge(sm("v4"), 512)

        def p4_sy(sy):
            for th in range(2):
                if th == 1:
                    sy.wait_ge(sm("t6"), 64)
                for j in range(4):
                    sy.dma_start(
                        au_res.ap().rearrange("p (a m) -> p a m", a=NDC)
                        [:, j * 8:(j + 1) * 8, :],
                        AU.rearrange("(a p) m -> p a m", p=128)
                        [:, j * 8:(j + 1) * 8, th * 1024:(th + 1) * 1024]).then_inc(
                            sm("l6a"), 16)
                for n in range(64):
                    ht, tt = divmod(n, 8)
                    g = th * 64 + n
                    r0 = th * 1024 + tt * 128
                    if g >= 2:
                        sy.wait_ge(sm("v6"), g - 1)
                    sy.dma_start(x4_b[g % 2][:],
                                 x_own[r0:r0 + 128, ht * 512:(ht + 1) * 512]).then_inc(
                                     psem("l6x", g), 16)
                    sy.wait_ge(sm("v6"), g + 1)
                    sy.dma_start(ATT[r0:r0 + 128, ht * 512:(ht + 1) * 512],
                                 at4_b[g % 2][:]).then_inc(psem("st6", g), 16)
            sy.wait_ge(psem("st6", 0), 16 * 64)
            sy.wait_ge(psem("st6", 1), 16 * 64)

        def p4_g(g):
            for th in range(2):
                for ht in range(8):
                    n = th * 8 + ht
                    if n >= 2:
                        g.wait_ge(sm("t6"), 8 * (n - 1))
                    g.dma_start(woc[n % 2][:],
                                wo.rearrange("(a p) m -> p a m", p=128)
                                [:, :, ht * 512:(ht + 1) * 512]).then_inc(psem("wl6", n), 16)

        def p4_t(t):
            for th in range(2):
                t.wait_ge(sm("l6a"), 16 * 4 * (th + 1))
                for n in range(64):
                    ht, tt = divmod(n, 8)
                    g = th * 64 + n
                    wn = th * 8 + ht
                    if tt == 0:
                        t.wait_ge(psem("wl6", wn), 16 * (wn // 2 + 1))
                    slot = PS6[g % 6]
                    if g >= 6:
                        t.wait_ge(sm("v6"), g - 5)
                    for fc in range(NDC):
                        mi = t.matmul(slot,
                                      au_res[:, fc * 1024 + tt * 128: fc * 1024 + tt * 128 + 128],
                                      woc[wn % 2][:, fc * 512:(fc + 1) * 512],
                                      start=(fc == 0), stop=(fc == NDC - 1))
                    mi.then_inc(sm("t6"), 1)

        def p4_v(v):
            for g in range(128):
                slot = PS6[g % 6]
                v.wait_ge(sm("t6"), g + 1)
                v.wait_ge(psem("l6x", g), 16 * (g // 2 + 1))
                if g >= 2:
                    v.wait_ge(psem("st6", g), 16 * (g // 2))
                v.tensor_tensor(at4_b[g % 2][:], slot, x4_b[g % 2][:],
                                op=ALU.add).then_inc(sm("v6"), 1)
            v.wait_ge(sm("v6"), 128)

        def p5a_sy(sy):
            for j in range(4):
                sy.dma_start(
                    xn5.ap().rearrange("p (a m) -> p a m", a=NDC)[:, j * 8:(j + 1) * 8, :],
                    xn_view(slice(j * 8, (j + 1) * 8), 0, T)).then_inc(sm("l7"), 16)
            sy.dma_start(gsc_sb[:], gsc.rearrange("o (c p) -> p (o c)", p=128),
                         allow_slow_non_contiguous=True).then_inc(sm("l7"), 16)
            sy.dma_start(usc_sb[:], usc.rearrange("o (c p) -> p (o c)", p=128),
                         allow_slow_non_contiguous=True).then_inc(sm("l7"), 16)
            n = 0
            for ic in range(NI):
                for tt in range(4):
                    sy.wait_ge(sm("v7"), n + 2)
                    sy.dma_start(GU[2 * tt, ic],
                                 gu5_b[n % 4][:, 0:256]).then_inc(psem("st7", n), 16)
                    sy.dma_start(GU[2 * tt + 1, ic],
                                 gu5_b[n % 4][:, 256:512]).then_inc(psem("st7", n), 16)
                    n += 1
            sy.wait_ge(psem("st7", 0), 32 * NI * 2)
            sy.wait_ge(psem("st7", 1), 32 * NI * 2)

        def p5a_g(g):
            for ic in range(NI):
                p = ic % 2
                if ic >= 2:
                    g.wait_ge(sm("wf7"), ic - 1)
                g.dma_start(wg5[p][:],
                            gate_t.rearrange("(a p) m -> p a m", p=128)
                            [:, :, ic * 128:(ic + 1) * 128]).then_inc(psem("wl7", ic), 16)
                g.dma_start(wu5[p][:],
                            up_t.rearrange("(a p) m -> p a m", p=128)
                            [:, :, ic * 128:(ic + 1) * 128]).then_inc(psem("wl7", ic), 16)

        def p5a_v(v):
            v.wait_ge(sm("l7"), 16 * 6)
            v.tensor_tensor(sc5_sb[:], gsc_sb[:], usc_sb[:],
                            op=ALU.mult).then_inc(sm("v7"), 1)
            n = 0
            for ic in range(NI):
                for tt in range(4):
                    pp = n % 2
                    v.wait_ge(sm("t7"), n + 1)
                    v.wait_ge(sm("a7"), n + 1)
                    if n >= 4:
                        v.wait_ge(psem("st7", n), 32 * (n // 2 - 1))
                    v.scalar_tensor_tensor(gu5_b[n % 4][:],
                                           (PA if pp == 0 else PBk)[:, 0:512],
                                           sc5_sb[:, ic:ic + 1], u5_b[pp][:],
                                           op0=ALU.mult, op1=ALU.mult).then_inc(sm("v7"), 1)
                    n += 1
            v.wait_ge(sm("v7"), NI * 4 + 1)

        def p5a_s(s):
            n = 0
            for ic in range(NI):
                for tt in range(4):
                    pp = n % 2
                    s.wait_ge(sm("t7"), n + 1)
                    s.activation(u5_b[pp][:], (PA if pp == 0 else PBk)[:, 512:1024],
                                 AF.Copy).then_inc(sm("a7"), 1)
                    n += 1
            s.wait_ge(sm("a7"), NI * 4)

        def p5a_t(t):
            t.wait_ge(sm("l7"), 16 * 4)
            n = 0
            for ic in range(NI):
                p = ic % 2
                t.wait_ge(psem("wl7", ic), 32 * (ic // 2 + 1))
                for tt in range(4):
                    pp = n % 2
                    gslot = (PA if pp == 0 else PBk)[:, 0:512]
                    uslot = (PA if pp == 0 else PBk)[:, 512:1024]
                    if n >= 2:
                        t.wait_ge(sm("v7"), n)
                        t.wait_ge(sm("a7"), n - 1)
                    for dc in range(NDC):
                        t.matmul(gslot, wg5[p][:, dc * 128:(dc + 1) * 128],
                                 xn5[:, dc * T + tt * 512: dc * T + tt * 512 + 512],
                                 start=(dc == 0), stop=(dc == NDC - 1))
                    for dc in range(NDC):
                        mi = t.matmul(uslot, wu5[p][:, dc * 128:(dc + 1) * 128],
                                      xn5[:, dc * T + tt * 512: dc * T + tt * 512 + 512],
                                      start=(dc == 0), stop=(dc == NDC - 1))
                    mi.then_inc(sm("t7"), 1)
                    n += 1
                t.nop().then_inc(sm("wf7"), 1)

        NHT = 16
        NTB = 8
        GPH = 2 * NTB

        def p5b_sy(sy):
            def gu_half(gb2):
                tb2 = gb2 % NTB
                if gb2 >= 2:
                    sy.wait_ge(sm("t8"), 2 * (gb2 - 1))
                for j in (2, 3):
                    lo = j * 22
                    hi = min(NI, lo + 22)
                    sy.dma_start(
                        gub[tb2 % 2].ap().rearrange("p (a m) -> p a m", a=NI)
                        [:, lo:hi, :],
                        GU[tb2].rearrange("a p m -> p a m")[:, lo:hi, :]).then_inc(
                            sems["gl8b" if gb2 % 2 == 0 else "gl8d"], 16)
            gu_half(0)
            n = 0
            for ht in range(NHT):
                for tb in range(NTB):
                    gb = ht * NTB + tb
                    if gb + 1 < NHT * NTB:
                        gu_half(gb + 1)
                    for ts in range(2):
                        r0 = tb * 256 + ts * 128
                        if n >= 2:
                            sy.wait_ge(sm("v8"), n - 1)
                        sy.dma_start(at5_b[n % 2][:],
                                     ATT[r0:r0 + 128, ht * 256:(ht + 1) * 256]).then_inc(
                                         psem("l8", n), 16)
                        sy.wait_ge(sm("v8"), n + 1)
                        sy.dma_start(out[r0:r0 + 128, ht * 256:(ht + 1) * 256],
                                     o5_b[n % 2][:]).then_inc(psem("st8", n), 16)
                        n += 1
            sy.wait_ge(psem("st8", 0), 16 * NHT * NTB)
            sy.wait_ge(psem("st8", 1), 16 * NHT * NTB)

        def p5b_g(g):
            for ht in range(NHT):
                if ht >= 1:
                    g.wait_ge(sm("v8d"), ht)
                g.dma_start(dscb_sl[:],
                            bass.AP(dsc.tensor, ht * 256, [[0, 128], [1, 256]]),
                            ).then_inc(sm("l8d"), 16)
                if ht >= 2:
                    g.wait_ge(sm("t8"), GPH * (ht - 1))
                for j in range(8):
                    lo = j * 11
                    hi = min(NI, lo + 11)
                    g.dma_start(
                        dcol[ht % 2].ap().rearrange("p (a m) -> p a m", a=NI)[:, lo:hi, :],
                        down_t.rearrange("(a p) m -> p a m", p=128)
                        [:, lo:hi, ht * 256:(ht + 1) * 256]).then_inc(psem("wl8", ht), 16)

        def p5b_s(s):
            for ht in range(NHT):
                for tb in range(NTB):
                    gb = ht * NTB + tb
                    if gb >= 2:
                        s.wait_ge(sm("t8"), 2 * (gb - 1))
                    for j in (0, 1):
                        lo = j * 22
                        hi = min(NI, lo + 22)
                        s.dma_start(
                            gub[tb % 2].ap().rearrange("p (a m) -> p a m", a=NI)
                            [:, lo:hi, :],
                            GU[tb].rearrange("a p m -> p a m")[:, lo:hi, :]).then_inc(
                                sems["gl8" if gb % 2 == 0 else "gl8c"], 16)

        def p5b_t(t):
            qsplit = [0, 22, 44, 66, NI]
            for ht in range(NHT):
                t.wait_ge(sm("v8d"), ht + 1)
                for tb in range(NTB):
                    gb = ht * NTB + tb
                    for ts in range(2):
                        n = 2 * gb + ts
                        slot = PS6[n % 6][:, 0:256]
                        if n >= 6:
                            t.wait_ge(sm("v8"), n - 5)
                        t.wait_ge(sems["gl8" if gb % 2 == 0 else "gl8c"],
                                  32 * (gb // 2 + 1))
                        t.wait_ge(sems["gl8b" if gb % 2 == 0 else "gl8d"],
                                  32 * (gb // 2 + 1))
                        for jq in range(4):
                            for ic in range(qsplit[jq], qsplit[jq + 1]):
                                mi = t.matmul(slot,
                                              gub[tb % 2][:, ic * 256 + ts * 128:
                                                           ic * 256 + ts * 128 + 128],
                                              dcol[ht % 2][:, ic * 256:(ic + 1) * 256],
                                              start=(ic == 0), stop=(ic == NI - 1))
                        mi.then_inc(sm("t8"), 1)

        def p5b_v(v):
            def fold(ht2):
                v.wait_ge(sm("l8d"), 16 * (ht2 + 1))
                v.wait_ge(psem("wl8", ht2), 16 * 8 * (ht2 // 2 + 1))
                _dap = dscb_sl.ap()
                v.tensor_tensor(
                    dcol[ht2 % 2].ap().rearrange("p (a m) -> p a m", a=NI),
                    dcol[ht2 % 2].ap().rearrange("p (a m) -> p a m", a=NI),
                    bass.AP(_dap.tensor, _dap.offset,
                            [_dap.ap[0], [0, NI], _dap.ap[1]]),
                    op=ALU.mult).then_inc(sm("v8d"), 1)
            fold(0)
            n = 0
            for ht in range(NHT):
                for tb in range(NTB):
                    for ts in range(2):
                        slot = PS6[n % 6][:, 0:256]
                        v.wait_ge(sm("t8"), n + 1)
                        v.wait_ge(psem("l8", n), 16 * (n // 2 + 1))
                        if n >= 2:
                            v.wait_ge(psem("st8", n), 16 * (n // 2))
                        v.tensor_tensor(o5_b[n % 2][:], slot, at5_b[n % 2][:],
                                        op=ALU.add).then_inc(sm("v8"), 1)
                        n += 1
                        if tb == 5 and ts == 1 and ht + 1 < NHT:
                            fold(ht + 1)
            v.wait_ge(sm("v8"), NHT * NTB * 2)

        P = phase_limit

        @block.gpsimd
        def _(g):
            ph = 0
            if P > 0: p0_g(g); ph += 1; barrier(g, ph)
            if P > 1: p1_g(g); ph += 1; barrier(g, ph)
            if P > 2: p2q_g(g); ph += 1; barrier(g, ph)
            if P > 3: p2l_g(g); ph += 1; barrier(g, ph)
            if P > 4: p3_g(g); ph += 1; barrier(g, ph)
            if P > 5: p4_g(g); ph += 1; barrier(g, ph)
            if P > 6: p5a_g(g); ph += 1; barrier(g, ph)
            if P > 7: p5b_g(g); ph += 1; barrier(g, ph)

        @block.sync
        def _(sy):
            ph = 0
            if P > 0: ph += 1; barrier(sy, ph)
            if P > 1: p1_sy(sy); ph += 1; barrier(sy, ph)
            if P > 2: p2q_sy(sy); ph += 1; barrier(sy, ph)
            if P > 3: p2l_sy(sy); ph += 1; barrier(sy, ph)
            if P > 4: p3_sy(sy); ph += 1; barrier(sy, ph)
            if P > 5: p4_sy(sy); ph += 1; barrier(sy, ph)
            if P > 6: p5a_sy(sy); ph += 1; barrier(sy, ph)
            if P > 7: p5b_sy(sy); ph += 1; barrier(sy, ph)

        @block.vector
        def _(v):
            ph = 0
            if P > 0: p0_v(v); ph += 1; barrier(v, ph)
            if P > 1: p1_v(v); ph += 1; barrier(v, ph)
            if P > 2: p2q_v(v); ph += 1; barrier(v, ph)
            if P > 3: p2l_v(v); ph += 1; barrier(v, ph)
            if P > 4: p3_v(v); ph += 1; barrier(v, ph)
            if P > 5: p4_v(v); ph += 1; barrier(v, ph)
            if P > 6: p5a_v(v); ph += 1; barrier(v, ph)
            if P > 7: p5b_v(v); ph += 1; barrier(v, ph)

        @block.scalar
        def _(s):
            ph = 0
            if P > 0: p0_s(s); ph += 1; barrier(s, ph)
            if P > 1: p1_s(s); ph += 1; barrier(s, ph)
            if P > 2: p2q_s(s); ph += 1; barrier(s, ph)
            if P > 3: ph += 1; barrier(s, ph)
            if P > 4: p3_s(s); ph += 1; barrier(s, ph)
            if P > 5: ph += 1; barrier(s, ph)
            if P > 6: p5a_s(s); ph += 1; barrier(s, ph)
            if P > 7: p5b_s(s); ph += 1; barrier(s, ph)

        @block.tensor
        def _(t):
            ph = 0
            if P > 0: p0_t(t); ph += 1; barrier(t, ph)
            if P > 1: p1_t(t); ph += 1; barrier(t, ph)
            if P > 2: p2q_t(t); ph += 1; barrier(t, ph)
            if P > 3: p2l_t(t); ph += 1; barrier(t, ph)
            if P > 4: p3_t(t); ph += 1; barrier(t, ph)
            if P > 5: p4_t(t); ph += 1; barrier(t, ph)
            if P > 6: p5a_t(t); ph += 1; barrier(t, ph)
            if P > 7: p5b_t(t); ph += 1; barrier(t, ph)

    return nc


def make_in_maps(x, Wq, Wc, Wk, Wv, Wo, gwq, gs, uwq, us, dwq, ds):
    gate_tt = np.ascontiguousarray(np.asarray(gwq).T)
    up_tt = np.ascontiguousarray(np.asarray(uwq).T)
    down_tt = np.ascontiguousarray(np.asarray(dwq).T)
    shared = dict(
        wq=np.ascontiguousarray(Wq, np.float32), wc=np.ascontiguousarray(Wc, np.float32),
        wk=np.ascontiguousarray(Wk, np.float32), wv=np.ascontiguousarray(Wv, np.float32),
        wo=np.ascontiguousarray(Wo, np.float32),
        gate_t=gate_tt, up_t=up_tt, down_t=down_tt,
        gsc=np.ascontiguousarray(gs, np.float32).reshape(1, -1),
        usc=np.ascontiguousarray(us, np.float32).reshape(1, -1),
        dsc=np.ascontiguousarray(ds, np.float32).reshape(1, -1),
    )
    x = np.asarray(x, np.float32)
    in_maps = []
    for c in range(8):
        b, h = c // 2, c % 2
        m = dict(shared)
        m["x_own"] = np.ascontiguousarray(x[b, h * T:(h + 1) * T, :])
        m["x_oth"] = np.ascontiguousarray(x[b, (1 - h) * T:(2 - h) * T, :])
        in_maps.append(m)
    return in_maps


def kernel(L0_attn_hidden_states, L0_attn_Wq, L0_attn_Wc, L0_attn_Wk, L0_attn_Wv,
           L0_attn_Wo, L0_ffn_gate_wq, L0_ffn_gate_scale, L0_ffn_up_wq,
           L0_ffn_up_scale, L0_ffn_down_wq, L0_ffn_down_scale):
    in_maps = make_in_maps(L0_attn_hidden_states, L0_attn_Wq, L0_attn_Wc, L0_attn_Wk,
                           L0_attn_Wv, L0_attn_Wo, L0_ffn_gate_wq, L0_ffn_gate_scale,
                           L0_ffn_up_wq, L0_ffn_up_scale, L0_ffn_down_wq,
                           L0_ffn_down_scale)
    nc = build()
    res = run_bass_kernel_spmd(nc, in_maps, core_ids=list(range(8)))
    outp = np.empty((4, S, D), np.float32)
    for c in range(8):
        b, h = c // 2, c % 2
        outp[b, h * T:(h + 1) * T, :] = res.results[c]["out"]
    return outp
